# version 05
# baseline (speedup 1.0000x reference)
import numpy as np
import concourse.bass as bass
import concourse.mybir as mybir
from concourse.bass_utils import run_bass_kernel_spmd

F32 = mybir.dt.float32
BF16 = mybir.dt.bfloat16
A = mybir.AluOpType
AF = mybir.ActivationFunctionType
AX = mybir.AxisListType.X

ENGINES = ("sync", "scalar", "vector", "tensor", "gpsimd")
D = 1024
DT = 8
CH = 512
LN_EPS = 1e-5
EPS = 1e-6


class Prog:

    def __init__(self):
        self.ops = {e: [] for e in ENGINES}
        self.cnt = {}
        self.sem_names = []

    def _sem(self, name):
        if name not in self.cnt:
            self.cnt[name] = 0
            self.sem_names.append(name)

    def op(self, eng, fn, inc=None, drain=False):
        if drain:
            self.ops[eng].append(("drain",))
        self.ops[eng].append(("op", fn, inc))
        if inc is not None:
            s, n = inc
            self._sem(s)
            self.cnt[s] += n

    def inc_last(self, eng, sem, n=1):
        for i in range(len(self.ops[eng]) - 1, -1, -1):
            it = self.ops[eng][i]
            if it[0] == "op":
                assert it[2] is None, "op already has an inc"
                self._sem(sem)
                self.ops[eng][i] = ("op", it[1], (sem, n))
                self.cnt[sem] += n
                return
        raise RuntimeError("no op to attach inc")

    def wait(self, eng, sem, val):
        self._sem(sem)
        if val > 0:
            self.ops[eng].append(("wait", sem, val))

    def mark(self, sem):
        self._sem(sem)
        return self.cnt.get(sem, 0)

    def wait_all(self, eng, sem):
        self.wait(eng, sem, self.mark(sem))

    def replay(self, eng_name, eng, sems):
        for item in self.ops[eng_name]:
            if item[0] == "drain":
                eng.drain()
            elif item[0] == "wait":
                eng.wait_ge(sems[item[1]], item[2])
            else:
                _, fn, inc = item
                ins = fn(eng)
                if inc is not None:
                    ins.then_inc(sems[inc[0]], inc[1])


def build_nc(T=2048):
    NSUB = T // 128
    NCH = T // CH
    SPC = CH // 128

    nc = bass.Bass("TRN2", target_bir_lowering=False, debug=False)

    xseg = nc.dram_tensor("xseg", [T, D], F32, kind="ExternalInput").ap()
    wq_t = nc.dram_tensor("wq_t", [D, D], F32, kind="ExternalInput").ap()
    wk_t = nc.dram_tensor("wk_t", [D, D], F32, kind="ExternalInput").ap()
    wv_t = nc.dram_tensor("wv_t", [D, D], F32, kind="ExternalInput").ap()
    wo_t = nc.dram_tensor("wo_t", [D, D], F32, kind="ExternalInput").ap()
    basew_t = nc.dram_tensor("basew_t", [D, D], F32, kind="ExternalInput").ap()
    memw = nc.dram_tensor("memw", [D, D], F32, kind="ExternalInput").ap()
    mn_c = nc.dram_tensor("mn_c", [128, 8], F32, kind="ExternalInput").ap()
    gate_c = nc.dram_tensor("gate_c", [128, 1], F32, kind="ExternalInput").ap()
    acat_t = nc.dram_tensor("acat_t", [D, 118], F32, kind="ExternalInput").ap()
    bcat = nc.dram_tensor("bcat", [113, D], F32, kind="ExternalInput").ap()
    rb_c = nc.dram_tensor("rb_c", [128, 8], F32, kind="ExternalInput").ap()
    pmask = nc.dram_tensor("pmask", [128, 4], F32, kind="ExternalInput").ap()
    y = nc.dram_tensor("y", [T, D], F32, kind="ExternalOutput").ap()

    scr_h1 = nc.dram_tensor("scr_h1", [T, D], BF16).ap()
    scr_q = nc.dram_tensor("scr_q", [T, D], BF16).ap()
    scr_k = nc.dram_tensor("scr_k", [T, D], BF16).ap()
    scr_oc = nc.dram_tensor("scr_oc", [T, D], BF16).ap()
    scr_h2 = nc.dram_tensor("scr_h2", [T, D], BF16).ap()
    scr_hp = nc.dram_tensor("scr_hp", [T, 128], BF16).ap()
    gz = nc.dram_tensor("gz", [1025, D], BF16).ap()
    gzall = nc.dram_tensor("gzall", [4 * 1025, D], BF16).ap()

    P = Prog()

    from contextlib import ExitStack
    ctx = ExitStack()
    with ctx:
        u_big = ctx.enter_context(nc.sbuf_tensor("u_big", [128, 16 * D], F32))
        u_v = ctx.enter_context(nc.sbuf_tensor("u_v", [128, 16 * D], BF16))
        u_a = ctx.enter_context(nc.sbuf_tensor("u_a", [128, 16 * D], BF16))
        u_b = ctx.enter_context(nc.sbuf_tensor("u_b", [128, 16 * D], BF16))
        u_c = ctx.enter_context(nc.sbuf_tensor("u_c", [128, 16 * D], BF16))
        masks = ctx.enter_context(nc.sbuf_tensor("masks", [128, 896], BF16))
        ident = ctx.enter_context(nc.sbuf_tensor("ident", [128, 128], BF16))
        ones_col = ctx.enter_context(nc.sbuf_tensor("ones_col", [128, 1], BF16))
        ones_row = ctx.enter_context(nc.sbuf_tensor("ones_row", [1, 128], BF16))
        me_sb = ctx.enter_context(nc.sbuf_tensor("me_sb", [128, D], BF16))
        mn_bf = ctx.enter_context(nc.sbuf_tensor("mn_bf", [128, 8], BF16))
        mn_f = ctx.enter_context(nc.sbuf_tensor("mn_f", [128, 8], F32))
        gate_sb = ctx.enter_context(nc.sbuf_tensor("gate_sb", [128, 1], F32))
        pm_sb = ctx.enter_context(nc.sbuf_tensor("pm_sb", [128, 4], F32))
        rb_sb = ctx.enter_context(nc.sbuf_tensor("rb_sb", [128, 8], F32))
        coef = ctx.enter_context(nc.sbuf_tensor("coef", [128, 8], F32))
        stat = ctx.enter_context(nc.sbuf_tensor("stat", [128, 32], F32))
        mnred_bf = ctx.enter_context(nc.sbuf_tensor("mnred_bf", [128, 1], BF16))
        act_bf = ctx.enter_context(nc.sbuf_tensor("act_bf", [1, 1], BF16))
        zbuf = ctx.enter_context(nc.sbuf_tensor("zbuf", [128, 64], F32))
        zbf = ctx.enter_context(nc.sbuf_tensor("zbf", [128, 32], BF16))
        zt_bf = ctx.enter_context(nc.sbuf_tensor("zt_bf", [128, 8], BF16))
        rrow = ctx.enter_context(nc.sbuf_tensor("rrow", [128, NSUB * 8], F32))
        rrow2 = ctx.enter_context(nc.sbuf_tensor("rrow2", [128, NSUB * 8], F32))
        rsml = ctx.enter_context(nc.sbuf_tensor("rsml", [128, NSUB * 3], F32))
        m8buf = ctx.enter_context(nc.sbuf_tensor("m8buf", [128, 8], F32))


        PA = ctx.enter_context(nc.psum_tensor("PA", [128, D], F32))
        PB = ctx.enter_context(nc.psum_tensor("PB", [128, D], F32))
        PC = ctx.enter_context(nc.psum_tensor("PC", [128, D], F32))
        ps_guard = nc.psum_tensor("PS", [128, D], F32)
        PS = ps_guard.__enter__()

        dg_v = u_big.ap().bitcast(BF16)

        def dG(c):
            return dg_v[:, 8192 * c: 8192 * (c + 1)]
        G_own = dg_v[:, 24576: 32768]
        ring_v = dg_v[:, 24576: 32768]

        def tmem_ring(i):
            return ring_v[:, 1024 * (i % 2): 1024 * (i % 2 + 1)]

        def oc_ring(i):
            return ring_v[:, 2048 + 1024 * (i % 4): 2048 + 1024 * (i % 4 + 1)]

        x_tm = u_big.ap()
        h1_tm = u_v.ap()
        v_tm = u_v.ap()
        h1_fm = u_a.ap()
        sq_fm = u_a.ap()
        oc_fm = u_a.ap()
        h2_fm = u_a.ap()
        wk_sb = u_b.ap()[:, 0:8192]
        wv_sb = u_b.ap()[:, 8192:16384]
        wq_sb = u_c.ap()[:, 0:8192]
        sk_fm = u_b.ap()
        mem_sb = u_b.ap()[:, 0:8192]
        wo_sb = u_b.ap()[:, 8192:16384]
        basew_sb = u_b.ap()[:, 0:8192]
        S_run = u_c.ap()[:, 0:8192]
        Ht_sb = u_c.ap()[:, 0:NSUB * 118]
        Hp_tm = u_c.ap()[:, 2048:2048 + NSUB * 128]
        Hp_fm = u_c.ap()[:, 4096:4096 + T]
        acat_sb = u_c.ap()[:, 6144:6144 + 8 * 118]
        bcat_sb = u_c.ap()[:, 7088:7088 + D]
        sqst_r = u_c.ap()[:, 8192:12288]
        skst_r = u_c.ap()[:, 12288:16384]
        AT_sb = u_c.ap()[:, 8192:16384]

        def hfm_ring(i):
            return u_c.ap()[:, 8192 + 1024 * (i % 2): 8192 + 1024 * (i % 2 + 1)]

        def h2st(i):
            return u_c.ap()[:, 12288 + 1024 * (i % 4): 12288 + 1024 * (i % 4 + 1)]

        def chunk_of(s_):
            return s_ // SPC

        P.op("gpsimd", lambda g: g.memset(masks[:], 1.0))
        P.op("gpsimd", lambda g: g.affine_select(
            masks[:], masks[:], pattern=[[1, 896]], compare_op=A.is_ge,
            fill=0.0, base=-384, channel_multiplier=-1), drain=True)
        P.op("gpsimd", lambda g: g.memset(ident[:], 1.0), drain=True)
        P.op("gpsimd", lambda g: g.affine_select(
            ident[:], ident[:], pattern=[[1, 128]], compare_op=A.is_equal,
            fill=0.0, base=0, channel_multiplier=-1), drain=True)
        P.op("gpsimd", lambda g: g.memset(ones_col[:], 1.0), drain=True)
        P.op("gpsimd", lambda g: g.memset(ones_row[:], 1.0), drain=True,
             inc=("GPC", 1))

        P.op("sync", lambda s: s.dma_start(out=mn_f[:], in_=mn_c), inc=("SYS", 16))
        P.op("sync", lambda s: s.dma_start(out=gate_sb[:], in_=gate_c),
             inc=("SYS", 16))
        P.op("sync", lambda s: s.dma_start(out=pm_sb[:], in_=pmask), inc=("SYS", 16))
        P.op("sync", lambda s: s.dma_start(out=rb_sb[:], in_=rb_c), inc=("SYS", 16))

        def wload(dst, src):
            return lambda g: g.dma_start(
                out=dst.rearrange("p (t e) -> p t e", t=DT),
                in_=src.rearrange("(t p) e -> p t e", p=128))
        P.op("gpsimd", wload(wk_sb, wk_t), inc=("GWKV", 16))
        P.op("gpsimd", wload(wv_sb, wv_t), inc=("GWKV", 16))
        P.op("gpsimd", lambda g: g.dma_start(out=mn_bf[:], in_=mn_c),
             inc=("GWKV", 16))
        P.wait("gpsimd", "SYH1a0", 64)
        P.wait("gpsimd", "SYH1b0", 64)
        P.op("gpsimd", wload(wq_sb, wq_t), inc=("GWQ", 16))

        for s_ in range(2 * SPC if NSUB > 2 * SPC else NSUB):
            qn = "sync" if chunk_of(s_) == 0 else "scalar"
            P.op(qn, lambda s, s_=s_: s.dma_start(
                out=x_tm[:, D * s_:D * (s_ + 1)],
                in_=xseg[128 * s_:128 * (s_ + 1), :]),
                inc=("SYX%d" % chunk_of(s_), 16))

        P.wait("vector", "SYS", 64)
        P.wait("scalar", "SYS", 64)
        P.op("scalar", lambda s: s.activation(out=coef[:, 0:1], in_=gate_sb[:],
                                              func=AF.Sigmoid), inc=("SA", 1))
        P.wait("vector", "SA", P.mark("SA"))
        P.op("vector", lambda v: v.tensor_scalar(coef[:, 1:2], coef[:, 0:1],
                                                 -1.0, 1.0, A.mult, A.add),
             drain=True)
        P.op("vector", lambda v: v.reduce_sum(stat[:, 23:24], mn_f[:], axis=AX),
             drain=True)
        P.op("vector", lambda v: v.tensor_copy(mnred_bf[:], stat[:, 23:24]),
             drain=True, inc=("VE", 1))
        P.wait("tensor", "VE", P.mark("VE"))
        P.wait("tensor", "GPC", 1)
        P.op("tensor", lambda t: t.matmul(PS[0:1, 16:17], mnred_bf[:], ones_col[:],
                                          start=True, stop=True), inc=("PE", 1))
        P.wait("vector", "PE", P.mark("PE"))
        P.op("vector", lambda v: v.tensor_scalar(act_bf[0:1, :], PS[0:1, 16:17],
                                                 float(EPS), None, A.is_ge),
             drain=True, inc=("VE", 1))
        P.wait("tensor", "VE", P.mark("VE"))
        P.op("tensor", lambda t: t.matmul(PS[:, 24:25], ones_row[:], act_bf[0:1, :],
                                          start=True, stop=True), inc=("PE", 1))
        P.wait("vector", "PE", P.mark("PE"))
        P.op("vector", lambda v: v.tensor_copy(coef[:, 2:3], PS[:, 24:25]),
             drain=True)
        P.op("vector", lambda v: v.tensor_tensor(coef[:, 3:4], coef[:, 0:1],
                                                 coef[:, 2:3], A.mult), drain=True,
             inc=("VE", 1))
        mk_coef = P.mark("VE")

        def ln_cols(par):
            b = 16 * par
            return (stat[:, b:b + 6], stat[:, b + 6:b + 12],
                    stat[:, b + 12:b + 14], stat[:, b + 14:b + 15],
                    stat[:, b + 15:b + 16])

        def emit_ln_a(xin, par, wait_ve=None):
            c0, c1, mv, t_, r_ = ln_cols(par)
            if wait_ve is not None:
                P.wait("vector", "VE", wait_ve)
            P.op("vector", lambda v: v.bn_stats(c0, xin[:, 0:512]), drain=True)
            P.op("vector", lambda v: v.bn_stats(c1, xin[:, 512:1024]), drain=True)
            P.op("vector", lambda v: v.bn_aggr(mv, stat[:, 16 * par:16 * par + 12]),
                 drain=True)
            P.op("vector", lambda v: v.tensor_scalar_add(t_, mv[:, 1:2], LN_EPS),
                 drain=True, inc=("VE", 1))
            return P.mark("VE")

        def emit_ln_b(par, mk_a):
            c0, c1, mv, t_, r_ = ln_cols(par)
            P.wait("scalar", "VE", mk_a)
            P.op("scalar", lambda s: s.activation(out=t_, in_=t_, func=AF.Ln),
                 drain=True)
            P.op("scalar", lambda s: s.activation(out=r_, in_=t_, func=AF.Exp,
                                                  scale=-0.5), drain=True,
                 inc=("SA", 1))
            return P.mark("SA")

        def emit_ln_c(xin, out_ap, par, mk_b, wait_ve=None):
            c0, c1, mv, t_, r_ = ln_cols(par)
            P.wait("vector", "SA", mk_b)
            if wait_ve is not None:
                P.wait("vector", "VE", wait_ve)
            P.op("vector", lambda v: v.tensor_scalar(
                out_ap, xin, mv[:, 0:1], r_, A.subtract, A.mult),
                drain=True, inc=("VE", 1))
            return P.mark("VE")

        pend = None
        for s_ in range(NSUB):
            c = chunk_of(s_)
            if s_ % SPC == 0:
                if c + 2 < NCH:
                    for si in range(SPC * (c + 2), SPC * (c + 3)):
                        P.op("scalar", lambda s, si=si: s.dma_start(
                            out=x_tm[:, D * si:D * (si + 1)],
                            in_=xseg[128 * si:128 * (si + 1), :]),
                            inc=("SYX%d" % (c + 2), 16))
                P.wait("vector", "SYX%d" % c, 64)
            mka = emit_ln_a(x_tm[:, D * s_:D * (s_ + 1)], s_ % 2)
            mkb = emit_ln_b(s_ % 2, mka)
            if pend is not None:
                ps_, pmk = pend
                mk = emit_ln_c(x_tm[:, D * ps_:D * (ps_ + 1)],
                               h1_tm[:, D * ps_:D * (ps_ + 1)], ps_ % 2, pmk)
                P.wait("scalar", "VE", mk)
                P.op("scalar", lambda s, ps_=ps_: s.dma_start(
                    out=scr_h1[128 * ps_:128 * (ps_ + 1), :],
                    in_=h1_tm[:, D * ps_:D * (ps_ + 1)]),
                    inc=("SH1%d" % chunk_of(ps_), 16))
            pend = (s_, mkb)
        ps_, pmk = pend
        mk = emit_ln_c(x_tm[:, D * ps_:D * (ps_ + 1)],
                       h1_tm[:, D * ps_:D * (ps_ + 1)], ps_ % 2, pmk)
        P.wait("scalar", "VE", mk)
        P.op("scalar", lambda s, ps_=ps_: s.dma_start(
            out=scr_h1[128 * ps_:128 * (ps_ + 1), :],
            in_=h1_tm[:, D * ps_:D * (ps_ + 1)]),
            inc=("SH1%d" % chunk_of(ps_), 16))

        for c in range(NCH):
            P.wait("sync", "SH1%d" % c, 64)
            for dt in range(4):
                P.op("sync", lambda s, c=c, dt=dt: s.dma_start(
                    out=h1_fm[:, 2048 * dt + CH * c: 2048 * dt + CH * (c + 1)],
                    in_=scr_h1[CH * c:CH * (c + 1), 128 * dt:128 * (dt + 1)],
                    transpose=True), inc=("SYH1a%d" % c, 16))
            P.wait("scalar", "SH1%d" % c, 64)
            for dt in range(4, DT):
                P.op("scalar", lambda s, c=c, dt=dt: s.dma_start(
                    out=h1_fm[:, 2048 * dt + CH * c: 2048 * dt + CH * (c + 1)],
                    in_=scr_h1[CH * c:CH * (c + 1), 128 * dt:128 * (dt + 1)],
                    transpose=True), inc=("SYH1b%d" % c, 16))

        P.wait("tensor", "GWKV", 48)
        P.wait("tensor", "GPC", 1)
        q_ev = {}
        k_ev = {}
        v_ev = {}
        prev_evac = {0: (0, 0), 1: (0, 0)}
        last_stt = 0
        kv_pe_c = {}
        for c in range(NCH):
            P.wait("tensor", "SYH1a%d" % c, 64)
            P.wait("tensor", "SYH1b%d" % c, 64)
            for k_ in range(SPC):
                s_ = SPC * c + k_
                kst = skst_r[:, 1024 * k_:1024 * (k_ + 1)]
                for h in range(2):
                    hs = slice(512 * h, 512 * (h + 1))
                    P.wait("tensor", "VE", prev_evac[h][0])
                    P.wait("tensor", "SA", prev_evac[h][1])
                    for dt in range(DT):
                        lo = 2048 * dt + 128 * s_
                        lhsT = h1_fm[:, lo:lo + 128]
                        for (ps, w_sb) in ((PB, wk_sb), (PC, wv_sb)):
                            P.op("tensor", lambda t, ps=ps, lhsT=lhsT, w_sb=w_sb,
                                 dt=dt, h=h, hs=hs: t.matmul(
                                     ps[:, hs], lhsT,
                                     w_sb[:, 1024 * dt + 512 * h:
                                          1024 * dt + 512 * (h + 1)],
                                     start=(dt == 0), stop=(dt == DT - 1)))
                    P.inc_last("tensor", "PE", 1)
                    mk_pe = P.mark("PE")
                    P.wait("scalar", "PE", mk_pe)
                    P.wait("scalar", "VE", last_stt)
                    P.op("scalar", lambda s, hs=hs: s.activation(
                        out=me_sb[:, 0:512], in_=PB[:, hs], func=AF.Relu,
                        scale=-1.0), drain=True)
                    P.op("scalar", lambda s: s.activation(
                        out=me_sb[:, 512:1024], in_=me_sb[:, 0:512], func=AF.Exp,
                        scale=-1.0), drain=True, inc=("SA", 1))
                    P.wait("vector", "SA", P.mark("SA"))
                    P.wait("vector", "PE", mk_pe)
                    if c > 0 and h == 0 and k_ == 0:
                        P.wait("vector", "SKW%d" % (c - 1), 64)
                    P.op("vector", lambda v, kst=kst, hs=hs: v.scalar_tensor_tensor(
                        kst[:, hs], PB[:, hs], 0.0, me_sb[:, 512:1024],
                        A.max, A.add), drain=True, inc=("VE", 1))
                    k_ev[(s_, h)] = P.mark("VE")
                    last_stt = k_ev[(s_, h)]
                    if h == 0:
                        P.wait("scalar", "SH1%d" % c, 64)
                    P.op("scalar", lambda s, s_=s_, hs=hs: s.activation(
                        out=v_tm[:, D * s_:D * (s_ + 1)][:, hs],
                        in_=PC[:, hs], func=AF.Copy), drain=True, inc=("SA", 1))
                    v_ev[(s_, h)] = P.mark("SA")
                    prev_evac[h] = (k_ev[(s_, h)], v_ev[(s_, h)])
                P.wait("scalar", "VE", k_ev[(s_, 1)])
                P.op("scalar", lambda s, s_=s_, kst=kst: s.dma_start(
                    out=scr_k[128 * s_:128 * (s_ + 1), :], in_=kst),
                    inc=("SKW%d" % c, 16))
            P.wait("tensor", "SA", v_ev[(SPC * c + SPC - 1, 1)])
            P.wait("tensor", "VE", k_ev[(SPC * c + SPC - 1, 1)])
            dg_marks = {}
            for dt in range(DT):
                ps = PC if dt % 2 == 0 else PA
                for j in range(SPC):
                    lhsT = skst_r[:, 1024 * j + 128 * dt:1024 * j + 128 * (dt + 1)]
                    sj = SPC * c + j
                    for h in range(2):
                        P.op("tensor", lambda t, ps=ps, lhsT=lhsT, sj=sj, h=h, j=j:
                             t.matmul(ps[:, 512 * h:512 * (h + 1)], lhsT,
                                      v_tm[:, D * sj + 512 * h:
                                           D * sj + 512 * (h + 1)],
                                      start=(j == 0), stop=(j == SPC - 1)))
                    P.op("tensor", lambda t, lhsT=lhsT, dt=dt, j=j: t.matmul(
                        PS[:, dt:dt + 1], lhsT, ones_col[:],
                        start=(j == 0), stop=(j == SPC - 1)))
                P.inc_last("tensor", "PE", 1)
                mk = P.mark("PE")
                if c < NCH - 1:
                    P.wait("scalar", "PE", mk)
                    P.op("scalar", lambda s, c=c, dt=dt, ps=ps: s.activation(
                        out=dG(c)[:, 1024 * dt:1024 * (dt + 1)], in_=ps[:],
                        func=AF.Copy), drain=True, inc=("SA", 1))
                    P.wait("vector", "SA", P.mark("SA"))
                    if c == 0:
                        P.op("vector", lambda v, c=c, dt=dt: v.tensor_copy(
                            G_own[:, 1024 * dt:1024 * (dt + 1)],
                            dG(c)[:, 1024 * dt:1024 * (dt + 1)]), drain=True,
                            inc=("VE", 1))
                    else:
                        P.op("vector", lambda v, c=c, dt=dt: v.tensor_tensor(
                            G_own[:, 1024 * dt:1024 * (dt + 1)],
                            G_own[:, 1024 * dt:1024 * (dt + 1)],
                            dG(c)[:, 1024 * dt:1024 * (dt + 1)], A.add),
                            drain=True, inc=("VE", 1))
                else:
                    P.wait("vector", "PE", mk)
                    if c == 0:
                        P.op("vector", lambda v, dt=dt, ps=ps: v.tensor_copy(
                            G_own[:, 1024 * dt:1024 * (dt + 1)], ps[:]),
                            drain=True, inc=("VE", 1))
                    else:
                        P.op("vector", lambda v, dt=dt, ps=ps: v.tensor_tensor(
                            G_own[:, 1024 * dt:1024 * (dt + 1)],
                            G_own[:, 1024 * dt:1024 * (dt + 1)], ps[:], A.add),
                            drain=True, inc=("VE", 1))
                dg_marks[dt % 2] = (P.mark("VE"), P.mark("SA"))
                if dt + 1 < DT:
                    nxt = dg_marks.get((dt + 1) % 2)
                    if nxt is not None:
                        P.wait("tensor", "SA", nxt[1])
                        P.wait("tensor", "VE", nxt[0])
            P.wait("vector", "PE", P.mark("PE"))
            P.op("vector", lambda v, c=c: v.tensor_copy(
                zbuf[:, 8 * (c + 1):8 * (c + 2)], PS[:, 0:8]), drain=True,
                inc=("VE", 1))
            prev_evac = {0: (P.mark("VE"), P.mark("SA")),
                         1: (P.mark("VE"), P.mark("SA"))}
            kv_pe_c[c] = P.mark("PE")
        mk_kv_pe = P.mark("PE")

        P.op("vector", lambda v: v.tensor_copy(zbuf[:, 40:48], zbuf[:, 8:16]),
             drain=True)
        for c in range(1, NCH):
            P.op("vector", lambda v, c=c: v.tensor_tensor(
                zbuf[:, 40:48], zbuf[:, 40:48],
                zbuf[:, 8 * (c + 1):8 * (c + 2)], A.add), drain=True)
        P.op("vector", lambda v: v.tensor_copy(zt_bf[:], zbuf[:, 40:48]),
             drain=True, inc=("VE", 1))
        P.wait("gpsimd", "VE", P.mark("VE"))
        P.op("gpsimd", lambda g: g.dma_start(
            out=gz[0:1024, :].rearrange("(t p) e -> p t e", p=128),
            in_=G_own.rearrange("p (t e) -> p t e", t=DT)), inc=("GGZ", 16))
        P.op("gpsimd", lambda g: g.dma_start(
            out=gz[1024:1025, :].rearrange("r (t p) -> r t p", t=8).transpose(
                [2, 0, 1]),
            in_=zt_bf[:].rearrange("p (r t) -> p r t", r=1),
            allow_slow_non_contiguous=True), inc=("GGZ", 16))
        P.wait("gpsimd", "GGZ", P.mark("GGZ"))
        P.op("gpsimd", lambda g: g.collective_compute(
            "AllGather", A.bypass, replica_groups=[[0, 1, 2, 3], [4, 5, 6, 7]],
            ins=[gz.opt()], outs=[gzall.opt()]), inc=("COL", 1))

        P.wait("tensor", "GWQ", 16)
        mk_kv_ve = P.mark("VE")
        prev_q = {0: (mk_kv_ve, 0), 1: (mk_kv_ve, 0)}
        q_pe_c = {}
        for c in range(NCH):
            for k_ in range(SPC):
                s_ = SPC * c + k_
                qst = sqst_r[:, 1024 * k_:1024 * (k_ + 1)]
                for h in range(2):
                    hs = slice(512 * h, 512 * (h + 1))
                    P.wait("tensor", "VE", prev_q[h][0])
                    for dt in range(DT):
                        lo = 2048 * dt + 128 * s_
                        lhsT = h1_fm[:, lo:lo + 128]
                        P.op("tensor", lambda t, lhsT=lhsT, dt=dt, h=h, hs=hs:
                             t.matmul(PA[:, hs], lhsT,
                                      wq_sb[:, 1024 * dt + 512 * h:
                                            1024 * dt + 512 * (h + 1)],
                                      start=(dt == 0), stop=(dt == DT - 1)))
                    P.inc_last("tensor", "PE", 1)
                    mk_pe = P.mark("PE")
                    P.wait("scalar", "PE", mk_pe)
                    P.wait("scalar", "VE", last_stt)
                    P.op("scalar", lambda s, hs=hs: s.activation(
                        out=me_sb[:, 0:512], in_=PA[:, hs], func=AF.Relu,
                        scale=-1.0), drain=True)
                    P.op("scalar", lambda s: s.activation(
                        out=me_sb[:, 512:1024], in_=me_sb[:, 0:512], func=AF.Exp,
                        scale=-1.0), drain=True, inc=("SA", 1))
                    P.wait("vector", "SA", P.mark("SA"))
                    P.wait("vector", "PE", mk_pe)
                    if c > 0 and h == 0 and k_ == 0:
                        P.wait("vector", "SQW%d" % (c - 1), 64)
                    P.op("vector", lambda v, qst=qst, hs=hs: v.scalar_tensor_tensor(
                        qst[:, hs], PA[:, hs], 0.0, me_sb[:, 512:1024],
                        A.max, A.add), drain=True, inc=("VE", 1))
                    q_ev[(s_, h)] = P.mark("VE")
                    last_stt = q_ev[(s_, h)]
                    prev_q[h] = (q_ev[(s_, h)], 0)
                P.wait("scalar", "VE", q_ev[(s_, 1)])
                P.op("scalar", lambda s, s_=s_, qst=qst: s.dma_start(
                    out=scr_q[128 * s_:128 * (s_ + 1), :], in_=qst),
                    inc=("SQW%d" % c, 16))
            q_pe_c[c] = P.mark("PE")
            P.wait("scalar", "PE", q_pe_c[c])
            P.wait("scalar", "SQW%d" % c, 64)
            for dt in range(DT):
                P.op("scalar", lambda s, c=c, dt=dt: s.dma_start(
                    out=sq_fm[:, 2048 * dt + CH * c: 2048 * dt + CH * (c + 1)],
                    in_=scr_q[CH * c:CH * (c + 1), 128 * dt:128 * (dt + 1)],
                    transpose=True), inc=("SQRD%d" % c, 16))
        mk_p2_ve = P.mark("VE")
        mk_p2_pe = P.mark("PE")

        P.wait("sync", "PE", mk_kv_pe)
        for c in range(NCH):
            P.wait("sync", "SKW%d" % c, 64)
            for dt in range(DT):
                P.op("sync", lambda s, c=c, dt=dt: s.dma_start(
                    out=sk_fm[:, 2048 * dt + CH * c: 2048 * dt + CH * (c + 1)],
                    in_=scr_k[CH * c:CH * (c + 1), 128 * dt:128 * (dt + 1)],
                    transpose=True), inc=("SYKQ%d" % c, 16))
        for c in range(NCH):
            P.wait("vector", "SQW%d" % c, 64)
            P.wait("vector", "SKW%d" % c, 64)
        at_bank = {0: mk_p2_ve, 1: mk_p2_ve}
        at_done = {}
        for c in range(NCH):
            P.wait("tensor", "SYKQ%d" % c, 128)
            P.wait("tensor", "SQRD%d" % c, 128)
            for j in range(SPC):
                bank = j % 2
                ps = PA if bank == 0 else PB
                P.wait("tensor", "VE", at_bank[bank])
                for dt in range(DT):
                    P.op("tensor", lambda t, ps=ps, c=c, j=j, dt=dt: t.matmul(
                        ps[:, 0:512],
                        sk_fm[:, 2048 * dt + CH * c + 128 * j:
                              2048 * dt + CH * c + 128 * (j + 1)],
                        sq_fm[:, 2048 * dt + CH * c: 2048 * dt + CH * (c + 1)],
                        start=(dt == 0), stop=(dt == DT - 1)))
                P.inc_last("tensor", "PE", 1)
                P.wait("vector", "PE", P.mark("PE"))
                P.op("vector", lambda v, ps=ps, c=c, j=j: v.tensor_tensor(
                    AT_sb[:, 2048 * c + 512 * j: 2048 * c + 512 * (j + 1)],
                    ps[:, 0:512], masks[:, 384 - 128 * j:896 - 128 * j], A.mult),
                    drain=True, inc=("VE", 1))
                at_bank[bank] = P.mark("VE")
            at_done[c] = P.mark("VE")
        mk_at_pe = P.mark("PE")

        def gs_half(hf):
            return ring_v[:, 4096 * hf:4096 * (hf + 1)]

        P.wait("gpsimd", "COL", 1)
        P.wait("gpsimd", "GGZ", P.mark("GGZ"))
        P.wait("vector", "PE", mk_p2_pe)
        P.op("vector", lambda v: v.memset(S_run[:], 0.0), drain=True)
        P.op("vector", lambda v: v.memset(zbuf[:, 0:8], 0.0), drain=True,
             inc=("VE", 1))
        mk_pzero = P.mark("VE")
        half_ve = {0: mk_pzero, 1: mk_pzero}
        for r in range(3):
            P.op("gpsimd", lambda g, r=r: g.dma_start(
                out=zbf[:, 8 * r:8 * (r + 1)],
                in_=gzall[1025 * r + 1024:1025 * r + 1025, :].rearrange(
                    "rr (t p) -> rr t p", t=8).transpose([2, 0, 1]),
                allow_slow_non_contiguous=True), inc=("GPZ%d" % r, 16))
            for hf in range(2):
                P.wait("gpsimd", "VE", half_ve[hf])
                P.op("gpsimd", lambda g, r=r, hf=hf: g.dma_start(
                    out=gs_half(hf).rearrange("p (t e) -> p t e", t=4),
                    in_=gzall[1025 * r + 512 * hf:1025 * r + 512 * (hf + 1),
                              :].rearrange("(t p) e -> p t e", p=128)),
                    inc=("GPR%d_%d" % (r, hf), 16))
                P.wait("vector", "GPR%d_%d" % (r, hf), 16)
                P.op("vector", lambda v, r=r, hf=hf: v.scalar_tensor_tensor(
                    S_run[:, 4096 * hf:4096 * (hf + 1)], gs_half(hf)[:],
                    pm_sb[:, r:r + 1], S_run[:, 4096 * hf:4096 * (hf + 1)],
                    A.mult, A.add), drain=True, inc=("VE", 1))
                half_ve[hf] = P.mark("VE")
            P.wait("vector", "GPZ%d" % r, 16)
            P.op("vector", lambda v, r=r: v.scalar_tensor_tensor(
                zbuf[:, 0:8], zbf[:, 8 * r:8 * (r + 1)], pm_sb[:, r:r + 1],
                zbuf[:, 0:8], A.mult, A.add), drain=True, inc=("VE", 1))
        for c in range(NCH):
            if c > 0:
                P.op("vector", lambda v, c=c: v.tensor_tensor(
                    zbuf[:, 0:8], zbuf[:, 0:8], zbuf[:, 8 * c:8 * (c + 1)],
                    A.add), drain=True)
            P.op("vector", lambda v, c=c: v.tensor_copy(
                zbf[:, 8 * c:8 * (c + 1)], zbuf[:, 0:8]), drain=True)
        P.op("vector", lambda v: v.engine_nop(), inc=("VE", 1))
        mk_prefix = P.mark("VE")
        P.wait("gpsimd", "PE", mk_at_pe)
        P.op("gpsimd", wload(mem_sb, memw), inc=("GME", 16))
        P.op("gpsimd", wload(wo_sb, wo_t), inc=("GWO", 16))

        def oc_slot(i):
            sl = i % 4
            if sl == 0:
                return me_sb[:]
            return dg_v[:, 24576 + 1024 * (sl - 1):24576 + 1024 * sl]

        P.wait("tensor", "VE", mk_prefix)
        P.wait("tensor", "GME", 16)
        prev_num = {0: mk_prefix, 1: mk_prefix}
        mk_sadd = mk_prefix
        prev_t_sa = P.mark("SA")
        prev_stat_ve = mk_prefix
        P.wait("vector", "VE", mk_coef)
        scan_pe_c = {}
        for c in range(NCH):
            for k_ in range(SPC):
                s_ = SPC * c + k_
                nb = s_ % 2
                nps = PA if nb == 0 else PC
                P.wait("tensor", "VE", prev_num[nb])
                for j in range(k_ + 1):
                    lhsT = AT_sb[:, 2048 * c + 512 * j + 128 * k_:
                                 2048 * c + 512 * j + 128 * (k_ + 1)]
                    for h in range(2):
                        P.op("tensor", lambda t, nps=nps, lhsT=lhsT, c=c, j=j,
                             h=h: t.matmul(
                                 nps[:, 512 * h:512 * (h + 1)], lhsT,
                                 v_tm[:, D * (SPC * c + j) + 512 * h:
                                      D * (SPC * c + j) + 512 * (h + 1)],
                                 start=(j == 0), stop=False))
                P.wait("tensor", "SA", prev_t_sa)
                if k_ == 0 and c > 0:
                    for dt in range(DT):
                        lhsT = sq_fm[:, 2048 * dt + 128 * s_:
                                     2048 * dt + 128 * (s_ + 1)]
                        for h in range(2):
                            P.op("tensor", lambda t, lhsT=lhsT, dt=dt, h=h:
                                 t.matmul(PB[:, 512 * h:512 * (h + 1)], lhsT,
                                          mem_sb[:, 1024 * dt + 512 * h:
                                                 1024 * dt + 512 * (h + 1)],
                                          start=(dt == 0), stop=(dt == DT - 1)))
                    P.wait("tensor", "VE", mk_sadd)
                    for dt in range(DT):
                        lhsT = sq_fm[:, 2048 * dt + 128 * s_:
                                     2048 * dt + 128 * (s_ + 1)]
                        for h in range(2):
                            P.op("tensor", lambda t, nps=nps, lhsT=lhsT, dt=dt,
                                 h=h: t.matmul(
                                     nps[:, 512 * h:512 * (h + 1)], lhsT,
                                     S_run[:, 1024 * dt + 512 * h:
                                           1024 * dt + 512 * (h + 1)],
                                     start=False, stop=(dt == DT - 1)))
                else:
                    for dt in range(DT):
                        lhsT = sq_fm[:, 2048 * dt + 128 * s_:
                                     2048 * dt + 128 * (s_ + 1)]
                        for h in range(2):
                            P.op("tensor", lambda t, nps=nps, lhsT=lhsT, dt=dt,
                                 h=h: t.matmul(
                                     nps[:, 512 * h:512 * (h + 1)], lhsT,
                                     S_run[:, 1024 * dt + 512 * h:
                                           1024 * dt + 512 * (h + 1)],
                                     start=False, stop=(dt == DT - 1)))
                            P.op("tensor", lambda t, lhsT=lhsT, dt=dt, h=h:
                                 t.matmul(PB[:, 512 * h:512 * (h + 1)], lhsT,
                                          mem_sb[:, 1024 * dt + 512 * h:
                                                 1024 * dt + 512 * (h + 1)],
                                          start=(dt == 0), stop=(dt == DT - 1)))
                P.wait("tensor", "VE", prev_stat_ve)
                for j in range(k_ + 1):
                    lhsT = AT_sb[:, 2048 * c + 512 * j + 128 * k_:
                                 2048 * c + 512 * j + 128 * (k_ + 1)]
                    P.op("tensor", lambda t, lhsT=lhsT, j=j: t.matmul(
                        PS[:, 0:1], lhsT, ones_col[:], start=(j == 0), stop=False))
                for dt in range(DT):
                    lhsT = sq_fm[:, 2048 * dt + 128 * s_:2048 * dt + 128 * (s_ + 1)]
                    P.op("tensor", lambda t, lhsT=lhsT, c=c, dt=dt: t.matmul(
                        PS[:, 0:1], lhsT, zbf[:, 8 * c + dt:8 * c + dt + 1],
                        start=False, stop=(dt == DT - 1)))
                    P.op("tensor", lambda t, lhsT=lhsT, dt=dt: t.matmul(
                        PS[:, 512:513], lhsT, mn_bf[:, dt:dt + 1],
                        start=(dt == 0), stop=(dt == DT - 1)))
                P.inc_last("tensor", "PE", 1)
                mk = P.mark("PE")
                P.wait("vector", "PE", mk)
                P.op("vector", lambda v: v.tensor_scalar_max(
                    stat[:, 17:18], PS[:, 0:1], EPS), drain=True)
                P.op("vector", lambda v: v.reciprocal(stat[:, 18:19],
                                                      stat[:, 17:18]), drain=True)
                P.op("vector", lambda v: v.tensor_tensor(
                    stat[:, 19:20], stat[:, 18:19], coef[:, 1:2], A.mult),
                    drain=True)
                P.op("vector", lambda v: v.tensor_scalar_max(
                    stat[:, 20:21], PS[:, 512:513], EPS), drain=True)
                P.op("vector", lambda v: v.reciprocal(stat[:, 21:22],
                                                      stat[:, 20:21]), drain=True)
                P.op("vector", lambda v: v.tensor_tensor(
                    stat[:, 22:23], stat[:, 21:22], coef[:, 3:4], A.mult),
                    drain=True, inc=("VE", 1))
                prev_stat_ve = P.mark("VE")
                P.wait("scalar", "VE", prev_stat_ve)
                if c > 0:
                    P.wait("scalar", "SOC%d" % (c - 1), 64)
                osl = oc_slot(s_)
                P.op("scalar", lambda s, osl=osl: s.activation(
                    out=osl, in_=PB[:], func=AF.Copy, scale=stat[:, 22:23]),
                    drain=True, inc=("SA", 1))
                prev_t_sa = P.mark("SA")
                P.wait("vector", "SA", prev_t_sa)
                P.op("vector", lambda v, nps=nps, osl=osl: v.scalar_tensor_tensor(
                    osl, nps[:], stat[:, 19:20], osl, A.mult, A.add),
                    drain=True, inc=("VE", 1))
                prev_num[nb] = P.mark("VE")
                P.wait("scalar", "VE", prev_num[nb])
                P.op("scalar", lambda s, s_=s_, osl=osl: s.dma_start(
                    out=scr_oc[128 * s_:128 * (s_ + 1), :], in_=osl),
                    inc=("SOC%d" % c, 16))
            if c < NCH - 1:
                P.op("vector", lambda v, c=c: v.tensor_tensor(
                    S_run[:], S_run[:], dG(c), A.add), drain=True, inc=("VE", 1))
                mk_sadd = P.mark("VE")
                P.wait("sync", "VE", mk_sadd)
                for si in range(SPC * c, SPC * (c + 1)):
                    P.op("sync", lambda s, si=si: s.dma_start(
                        out=x_tm[:, D * si:D * (si + 1)],
                        in_=xseg[128 * si:128 * (si + 1), :]),
                        inc=("SYX2R%d" % c, 16))
            scan_pe_c[c] = P.mark("PE")
        mk_scan_ve = P.mark("VE")
        mk_scan_pe = P.mark("PE")
        mk_scan_sa = P.mark("SA")

        oc_rd = {}
        for c in range(NCH):
            P.wait("sync", "PE", scan_pe_c[c])
            P.wait("sync", "SOC%d" % c, 64)
            for dt in range(DT):
                P.op("sync", lambda s, c=c, dt=dt: s.dma_start(
                    out=oc_fm[:, 2048 * dt + CH * c: 2048 * dt + CH * (c + 1)],
                    in_=scr_oc[CH * c:CH * (c + 1), 128 * dt:128 * (dt + 1)],
                    transpose=True), inc=("SYOC%d" % c, 16))
        P.wait("sync", "GGZ", 32)
        P.wait("sync", "VE", mk_scan_ve)
        P.wait("sync", "SA", mk_scan_sa)
        P.wait("sync", "SOC%d" % (NCH - 1), 64)
        for si in range(SPC * (NCH - 1), NSUB):
            P.op("sync", lambda s, si=si: s.dma_start(
                out=x_tm[:, D * si:D * (si + 1)],
                in_=xseg[128 * si:128 * (si + 1), :]), inc=("SYX2L", 16))
        P.wait("gpsimd", "PE", mk_scan_pe)
        P.op("gpsimd", wload(basew_sb, basew_t), inc=("GFW", 16))
        P.op("gpsimd", lambda g: g.dma_start(
            out=acat_sb.rearrange("p (t r) -> p t r", t=DT),
            in_=acat_t.rearrange("(t p) r -> p t r", p=128)), inc=("GAB", 16))
        P.op("gpsimd", lambda g: g.dma_start(
            out=bcat_sb[0:113, :], in_=bcat), inc=("GAB", 16))

        def x2_wait_for(s_):
            cc = chunk_of(s_)
            if cc < NCH - 1:
                return ("SYX2R%d" % cc, 64)
            return ("SYX2L", 16 * SPC)

        P.wait("tensor", "GWO", 16)
        x2_done = {}
        oproj_pe_c = {}
        bank_mark = {0: mk_scan_ve, 1: mk_scan_ve}
        pend = None

        def _ln2_fin(ps_, pmk):
            c2 = chunk_of(ps_)
            if c2 > 0:
                P.wait("vector", "SH2%d" % (c2 - 1), 64)
            mk = emit_ln_c(x_tm[:, D * ps_:D * (ps_ + 1)], h2st(ps_), ps_ % 2,
                           pmk)
            P.wait("scalar", "VE", mk)
            P.op("scalar", lambda s, ps_=ps_: s.dma_start(
                out=scr_h2[128 * ps_:128 * (ps_ + 1), :], in_=h2st(ps_)),
                inc=("SH2%d" % c2, 16))

        for c in range(NCH):
            P.wait("tensor", "SYOC%d" % c, 128)
            for k_ in range(SPC):
                s_ = SPC * c + k_
                bank = s_ % 2
                ps = PA if bank == 0 else PB
                P.wait("tensor", "VE", bank_mark[bank])
                for et in range(DT):
                    lhsT = oc_fm[:, 2048 * et + 128 * s_:2048 * et + 128 * (s_ + 1)]
                    for h in range(2):
                        P.op("tensor", lambda t, ps=ps, lhsT=lhsT, et=et, h=h:
                             t.matmul(ps[:, 512 * h:512 * (h + 1)], lhsT,
                                      wo_sb[:, 1024 * et + 512 * h:
                                            1024 * et + 512 * (h + 1)],
                                      start=(et == 0), stop=(et == DT - 1)))
                P.inc_last("tensor", "PE", 1)
                P.wait("vector", "PE", P.mark("PE"))
                wsem, wval = x2_wait_for(s_)
                P.wait("vector", wsem, wval)
                P.op("vector", lambda v, s_=s_, ps=ps: v.tensor_tensor(
                    x_tm[:, D * s_:D * (s_ + 1)], x_tm[:, D * s_:D * (s_ + 1)],
                    ps[:], A.add), drain=True, inc=("VE", 1))
                x2_done[s_] = P.mark("VE")
                bank_mark[bank] = x2_done[s_]
                mka = emit_ln_a(x_tm[:, D * s_:D * (s_ + 1)], s_ % 2)
                mkb = emit_ln_b(s_ % 2, mka)
                if pend is not None:
                    _ln2_fin(*pend)
                pend = (s_, mkb)
            oproj_pe_c[c] = P.mark("PE")
            if c > 0:
                cc = c - 1
                P.wait("sync", "PE", oproj_pe_c[cc])
                P.wait("sync", "SH2%d" % cc, 64)
                for dt in range(DT):
                    P.op("sync", lambda s, cc=cc, dt=dt: s.dma_start(
                        out=h2_fm[:, 2048 * dt + CH * cc:
                                  2048 * dt + CH * (cc + 1)],
                        in_=scr_h2[CH * cc:CH * (cc + 1),
                                   128 * dt:128 * (dt + 1)],
                        transpose=True), inc=("SYH2%d" % cc, 16))
        _ln2_fin(*pend)
        cc = NCH - 1
        P.wait("sync", "PE", oproj_pe_c[cc])
        P.wait("sync", "SH2%d" % cc, 64)
        for dt in range(DT):
            P.op("sync", lambda s, cc=cc, dt=dt: s.dma_start(
                out=h2_fm[:, 2048 * dt + CH * cc: 2048 * dt + CH * (cc + 1)],
                in_=scr_h2[CH * cc:CH * (cc + 1), 128 * dt:128 * (dt + 1)],
                transpose=True), inc=("SYH2%d" % cc, 16))

        ps_guard.__exit__(None, None, None)
        pf_guard = nc.psum_tensor("PF", [128, 2048], BF16)
        PF = pf_guard.__enter__()

        P.wait("tensor", "GFW", 16)
        P.wait("tensor", "GAB", 32)
        HtV = Ht_sb.rearrange("p (s r) -> p s r", s=NSUB)
        rrV = rrow.rearrange("p (s r) -> p s r", s=NSUB)
        rr2V = rrow2.rearrange("p (s r) -> p s r", s=NSUB)
        HpV = Hp_tm.rearrange("p (s r) -> p s r", s=NSUB)
        ht_bank = {0: mk_scan_ve, 1: mk_scan_ve}
        h_sa_marks = {0: P.mark("SA"), 1: P.mark("SA")}
        for c in range(NCH):
            sl = slice(SPC * c, SPC * (c + 1))
            P.wait("tensor", "SYH2%d" % c, 128)
            hb = 512 * (c % 2)
            P.wait("tensor", "SA", h_sa_marks[c % 2])
            for dt in range(DT):
                P.op("tensor", lambda t, c=c, dt=dt, hb=hb: t.matmul(
                    PC[0:118, hb:hb + 512], acat_sb[:, 118 * dt:118 * (dt + 1)],
                    h2_fm[:, 2048 * dt + CH * c:2048 * dt + CH * (c + 1)],
                    start=(dt == 0), stop=(dt == DT - 1)))
            P.inc_last("tensor", "PE", 1)
            P.wait("scalar", "PE", P.mark("PE"))
            P.op("scalar", lambda s, c=c, hb=hb: s.activation(
                out=hfm_ring(c)[0:118, 0:512], in_=PC[0:118, hb:hb + 512],
                func=AF.Copy), drain=True, inc=("SA", 1))
            h_sa_marks[c % 2] = P.mark("SA")
            P.wait("tensor", "SA", h_sa_marks[c % 2])
            for k_ in range(SPC):
                s_ = SPC * c + k_
                bank = s_ % 2
                pf = PF[:, 1024 * bank:1024 * bank + 118]
                P.wait("tensor", "VE", ht_bank[bank])
                P.op("tensor", lambda t, c=c, k_=k_, pf=pf: t.transpose(
                    pf, hfm_ring(c)[0:118, 128 * k_:128 * (k_ + 1)],
                    ident[0:118, 0:118]), inc=("PE", 1))
                P.wait("vector", "PE", P.mark("PE"))
                P.op("vector", lambda v, s_=s_, pf=pf: v.tensor_copy(
                    Ht_sb[:, 118 * s_:118 * (s_ + 1)], pf), drain=True,
                    inc=("VE", 1))
                ht_bank[bank] = P.mark("VE")
            P.op("vector", lambda v, sl=sl: v.memset(rrV[:, sl, :], -1e9),
                 drain=True)
            P.op("vector", lambda v, sl=sl: v.tensor_tensor(
                rrV[:, sl, 0:6], HtV[:, sl, 112:118],
                rb_sb[:, 0:6].unsqueeze(1).to_broadcast([128, SPC, 6]), A.add),
                drain=True, inc=("VE", 1))
            P.wait("scalar", "VE", P.mark("VE"))
            P.op("scalar", lambda s, c=c: s.activation(
                out=rrow[:, 8 * SPC * c:8 * SPC * (c + 1)],
                in_=rrow[:, 8 * SPC * c:8 * SPC * (c + 1)], func=AF.Exp),
                drain=True, inc=("SA", 1))
            P.wait("vector", "SA", P.mark("SA"))
            css = slice(SPC * c, SPC * (c + 1))
            P.op("vector", lambda v, css=css: v.tensor_reduce(
                rsml[:, css].unsqueeze(2), rrV[:, css, :], axis=AX, op=A.add),
                drain=True)
            P.op("vector", lambda v, css=css: v.reciprocal(
                rsml[:, css], rsml[:, css]), drain=True)
            P.op("vector", lambda v, css=css: v.tensor_tensor(
                rrV[:, css, :], rrV[:, css, :],
                rsml[:, css].unsqueeze(2).to_broadcast([128, SPC, 8]), A.mult),
                drain=True)
            for s_ in range(SPC * c, SPC * (c + 1)):
                P.op("vector", lambda v, s_=s_: v.max(
                    out=m8buf[:], in_=rrow[:, 8 * s_:8 * (s_ + 1)]), drain=True)
                P.op("vector", lambda v, s_=s_: v.tensor_copy(
                    rsml[:, NSUB + s_:NSUB + s_ + 1], m8buf[:, 1:2]), drain=True)
            cms = slice(NSUB + SPC * c, NSUB + SPC * (c + 1))
            P.op("vector", lambda v, css=css, cms=cms: v.tensor_tensor(
                rr2V[:, css, :], rrV[:, css, :],
                rsml[:, cms].unsqueeze(2).to_broadcast([128, SPC, 8]), A.is_ge),
                drain=True)
            P.op("vector", lambda v, css=css: v.tensor_tensor(
                rr2V[:, css, :], rr2V[:, css, :], rrV[:, css, :], A.mult),
                drain=True)
            c2s = slice(2 * NSUB + SPC * c, 2 * NSUB + SPC * (c + 1))
            P.op("vector", lambda v, css=css, c2s=c2s: v.tensor_reduce(
                rsml[:, c2s].unsqueeze(2), rr2V[:, css, :], axis=AX, op=A.add),
                drain=True)
            P.op("vector", lambda v, c2s=c2s: v.tensor_scalar_add(
                rsml[:, c2s], rsml[:, c2s], EPS), drain=True)
            P.op("vector", lambda v, c2s=c2s: v.reciprocal(
                rsml[:, c2s], rsml[:, c2s]), drain=True)
            P.op("vector", lambda v, css=css, c2s=c2s: v.tensor_tensor(
                rr2V[:, css, :], rr2V[:, css, :],
                rsml[:, c2s].unsqueeze(2).to_broadcast([128, SPC, 8]), A.mult),
                drain=True)
            P.op("vector", lambda v, css=css: v.tensor_copy(
                HpV[:, css, 0:16], HtV[:, css, 0:16]), drain=True)
            for e_ in range(6):
                P.op("vector", lambda v, e_=e_, css=css: v.tensor_tensor(
                    HpV[:, css, 16 + 16 * e_:32 + 16 * e_],
                    HtV[:, css, 16 + 16 * e_:32 + 16 * e_],
                    rr2V[:, css, e_:e_ + 1].to_broadcast([128, SPC, 16]),
                    A.mult), drain=True)
            P.op("vector", lambda v, css=css: v.memset(HpV[:, css, 112:113], 1.0),
                 drain=True)
            P.op("vector", lambda v, css=css: v.memset(HpV[:, css, 113:128], 0.0),
                 drain=True, inc=("VE", 1))
            mk_hp = P.mark("VE")
            P.wait("scalar", "VE", mk_hp)
            P.op("scalar", lambda s, c=c, css=css: s.dma_start(
                out=scr_hp[CH * c:CH * (c + 1), :].rearrange(
                    "(s p) r -> p s r", p=128), in_=HpV[:, css, :]),
                inc=("SHP%d" % c, 16))
            P.wait("sync", "SHP%d" % c, 16)
            P.op("sync", lambda s, c=c: s.dma_start(
                out=Hp_fm[:, CH * c:CH * (c + 1)],
                in_=scr_hp[CH * c:CH * (c + 1), :], transpose=True),
                inc=("SYHP%d" % c, 16))

        bank_mark = {0: P.mark("VE"), 1: P.mark("VE")}
        for s_ in range(NSUB):
            bank = s_ % 2
            ps = PA if bank == 0 else PB
            if s_ % SPC == 0:
                P.wait("tensor", "SYHP%d" % chunk_of(s_), 16)
            P.wait("tensor", "VE", bank_mark[bank])
            for dt in range(DT):
                lhsT = h2_fm[:, 2048 * dt + 128 * s_:2048 * dt + 128 * (s_ + 1)]
                for h in range(2):
                    P.op("tensor", lambda t, ps=ps, lhsT=lhsT, dt=dt, h=h:
                         t.matmul(ps[:, 512 * h:512 * (h + 1)], lhsT,
                                  basew_sb[:, 1024 * dt + 512 * h:
                                           1024 * dt + 512 * (h + 1)],
                                  start=(dt == 0), stop=False))
            lhsT2 = Hp_fm[0:113, 128 * s_:128 * (s_ + 1)]
            for h in range(2):
                P.op("tensor", lambda t, ps=ps, lhsT2=lhsT2, h=h: t.matmul(
                    ps[:, 512 * h:512 * (h + 1)], lhsT2,
                    bcat_sb[0:113, 512 * h:512 * (h + 1)],
                    start=False, stop=True))
            P.inc_last("tensor", "PE", 1)
            P.wait("vector", "PE", P.mark("PE"))
            P.op("vector", lambda v, s_=s_, ps=ps: v.tensor_tensor(
                x_tm[:, D * s_:D * (s_ + 1)], x_tm[:, D * s_:D * (s_ + 1)],
                ps[:], A.add), drain=True, inc=("VE", 1))
            bank_mark[bank] = P.mark("VE")
            P.wait("sync", "VE", bank_mark[bank])
            P.op("sync", lambda s, s_=s_: s.dma_start(
                out=y[128 * s_:128 * (s_ + 1), :],
                in_=x_tm[:, D * s_:D * (s_ + 1)]), inc=("SYO", 16))
        P.wait("sync", "SYO", P.mark("SYO"))

        pf_guard.__exit__(None, None, None)

        sems = {}
        for name in P.sem_names:
            sems[name] = ctx.enter_context(nc.semaphore(name))

        block = ctx.enter_context(nc.Block())

        @block.sync
        def _(eng):
            P.replay("sync", eng, sems)

        @block.scalar
        def _(eng):
            P.replay("scalar", eng, sems)

        @block.vector
        def _(eng):
            P.replay("vector", eng, sems)

        @block.tensor
        def _(eng):
            P.replay("tensor", eng, sems)

        @block.gpsimd
        def _(eng):
            P.replay("gpsimd", eng, sems)

    return nc


def _stage_inputs(inputs, T=2048):
    x = np.asarray(inputs["L0_attn_hidden_states"], np.float32)
    tp = lambda w: np.ascontiguousarray(np.asarray(w, np.float32).T)
    mn = np.asarray(inputs["L0_attn_memory_norm"], np.float32)
    sa = np.asarray(inputs["L0_ffn_shared_A"], np.float32)
    ra = np.asarray(inputs["L0_ffn_routed_A"], np.float32)
    rw = np.asarray(inputs["L0_ffn_router_W"], np.float32)
    acat_t = np.ascontiguousarray(
        np.concatenate([sa, ra.reshape(96, 1024), rw], axis=0).T)
    sb = np.asarray(inputs["L0_ffn_shared_B"], np.float32)
    rbm = np.asarray(inputs["L0_ffn_routed_B"], np.float32)
    bb = np.asarray(inputs["L0_ffn_base_b"], np.float32)
    bcat = np.ascontiguousarray(np.concatenate(
        [np.ascontiguousarray(sb.T), rbm.transpose(0, 2, 1).reshape(96, 1024),
         bb[None, :]], axis=0))
    rbias = np.asarray(inputs["L0_ffn_router_b"], np.float32)
    rb_c = np.zeros((128, 8), np.float32)
    rb_c[:, 0:6] = rbias[None, :]
    shared = dict(
        wq_t=tp(inputs["L0_attn_w_q"]), wk_t=tp(inputs["L0_attn_w_k"]),
        wv_t=tp(inputs["L0_attn_w_v"]), wo_t=tp(inputs["L0_attn_w_o"]),
        basew_t=tp(inputs["L0_ffn_base_W"]),
        memw=np.ascontiguousarray(np.asarray(inputs["L0_attn_memory"],
                                             np.float32)),
        mn_c=np.ascontiguousarray(mn.reshape(8, 128).T),
        gate_c=np.ascontiguousarray(np.broadcast_to(
            np.asarray(inputs["L0_attn_gate"], np.float32).reshape(1, 1),
            (128, 1))),
        acat_t=acat_t, bcat=bcat, rb_c=rb_c)
    per = []
    for c in range(8):
        b, q = c // 4, c % 4
        pm = np.zeros((128, 4), np.float32)
        pm[:, 0:q] = 1.0
        m = dict(shared)
        m["xseg"] = np.ascontiguousarray(x[b, T * q:T * (q + 1), :])
        m["pmask"] = pm
        per.append(m)
    return per


def kernel(**inputs):
    T = 2048
    per = _stage_inputs(inputs, T)
    nc = build_nc(T)
    res = run_bass_kernel_spmd(nc, per, core_ids=list(range(8)))
    outs = [res.results[c]["y"] for c in range(8)]
    b0 = np.concatenate(outs[0:4], axis=0)
    b1 = np.concatenate(outs[4:8], axis=0)
    return np.stack([b0, b1], axis=0).astype(np.float32)


# version 07 (final)
# speedup vs baseline: 1.0244x; 1.0244x over previous
import numpy as np
import concourse.bass as bass
import concourse.mybir as mybir
from concourse.bass_utils import run_bass_kernel_spmd

F32 = mybir.dt.float32
BF16 = mybir.dt.bfloat16
A = mybir.AluOpType
AF = mybir.ActivationFunctionType
AX = mybir.AxisListType.X

ENGINES = ("sync", "scalar", "vector", "tensor", "gpsimd")
D = 1024
DT = 8
CH = 512
LN_EPS = 1e-5
EPS = 1e-6


class Prog:

    def __init__(self):
        self.ops = {e: [] for e in ENGINES}
        self.cnt = {}
        self.sem_names = []

    def _sem(self, name):
        if name not in self.cnt:
            self.cnt[name] = 0
            self.sem_names.append(name)

    def op(self, eng, fn, inc=None, drain=False):
        if drain:
            self.ops[eng].append(("drain",))
        self.ops[eng].append(("op", fn, inc))
        if inc is not None:
            s, n = inc
            self._sem(s)
            self.cnt[s] += n

    def inc_last(self, eng, sem, n=1):
        for i in range(len(self.ops[eng]) - 1, -1, -1):
            it = self.ops[eng][i]
            if it[0] == "op":
                assert it[2] is None, "op already has an inc"
                self._sem(sem)
                self.ops[eng][i] = ("op", it[1], (sem, n))
                self.cnt[sem] += n
                return
        raise RuntimeError("no op to attach inc")

    def wait(self, eng, sem, val):
        self._sem(sem)
        if val > 0:
            self.ops[eng].append(("wait", sem, val))

    def mark(self, sem):
        self._sem(sem)
        return self.cnt.get(sem, 0)

    def wait_all(self, eng, sem):
        self.wait(eng, sem, self.mark(sem))

    def replay(self, eng_name, eng, sems):
        for item in self.ops[eng_name]:
            if item[0] == "drain":
                eng.drain()
            elif item[0] == "wait":
                eng.wait_ge(sems[item[1]], item[2])
            else:
                _, fn, inc = item
                ins = fn(eng)
                if inc is not None:
                    ins.then_inc(sems[inc[0]], inc[1])


def build_nc(T=2048):
    NSUB = T // 128
    NCH = T // CH
    SPC = CH // 128

    nc = bass.Bass("TRN2", target_bir_lowering=False, debug=False)

    xseg = nc.dram_tensor("xseg", [T, D], F32, kind="ExternalInput").ap()
    wq_t = nc.dram_tensor("wq_t", [D, D], F32, kind="ExternalInput").ap()
    wk_t = nc.dram_tensor("wk_t", [D, D], F32, kind="ExternalInput").ap()
    wv_t = nc.dram_tensor("wv_t", [D, D], F32, kind="ExternalInput").ap()
    wo_t = nc.dram_tensor("wo_t", [D, D], F32, kind="ExternalInput").ap()
    basew_t = nc.dram_tensor("basew_t", [D, D], F32, kind="ExternalInput").ap()
    memw = nc.dram_tensor("memw", [D, D], F32, kind="ExternalInput").ap()
    mn_c = nc.dram_tensor("mn_c", [128, 8], F32, kind="ExternalInput").ap()
    gate_c = nc.dram_tensor("gate_c", [128, 1], F32, kind="ExternalInput").ap()
    acat_t = nc.dram_tensor("acat_t", [D, 118], F32, kind="ExternalInput").ap()
    bcat = nc.dram_tensor("bcat", [113, D], F32, kind="ExternalInput").ap()
    rb_c = nc.dram_tensor("rb_c", [128, 8], F32, kind="ExternalInput").ap()
    pmask = nc.dram_tensor("pmask", [128, 4], F32, kind="ExternalInput").ap()
    y = nc.dram_tensor("y", [T, D], F32, kind="ExternalOutput").ap()

    scr_h1 = nc.dram_tensor("scr_h1", [T, D], BF16).ap()
    scr_q = nc.dram_tensor("scr_q", [T, D], BF16).ap()
    scr_k = nc.dram_tensor("scr_k", [T, D], BF16).ap()
    scr_oc = nc.dram_tensor("scr_oc", [T, D], BF16).ap()
    scr_h2 = nc.dram_tensor("scr_h2", [T, D], BF16).ap()
    scr_hp = nc.dram_tensor("scr_hp", [T, 128], BF16).ap()
    gz = nc.dram_tensor("gz", [1025, D], BF16).ap()
    gzall = nc.dram_tensor("gzall", [4 * 1025, D], BF16).ap()

    P = Prog()

    from contextlib import ExitStack
    ctx = ExitStack()
    with ctx:
        u_big = ctx.enter_context(nc.sbuf_tensor("u_big", [128, 16 * D], F32))
        u_v = ctx.enter_context(nc.sbuf_tensor("u_v", [128, 16 * D], BF16))
        u_a = ctx.enter_context(nc.sbuf_tensor("u_a", [128, 16 * D], BF16))
        u_b = ctx.enter_context(nc.sbuf_tensor("u_b", [128, 16 * D], BF16))
        u_c = ctx.enter_context(nc.sbuf_tensor("u_c", [128, 16 * D], BF16))
        masks = ctx.enter_context(nc.sbuf_tensor("masks", [128, 896], BF16))
        ident = ctx.enter_context(nc.sbuf_tensor("ident", [128, 128], BF16))
        ones_col = ctx.enter_context(nc.sbuf_tensor("ones_col", [128, 1], BF16))
        ones_row = ctx.enter_context(nc.sbuf_tensor("ones_row", [1, 128], BF16))
        me_sb = ctx.enter_context(nc.sbuf_tensor("me_sb", [128, D], BF16))
        mn_bf = ctx.enter_context(nc.sbuf_tensor("mn_bf", [128, 8], BF16))
        mn_f = ctx.enter_context(nc.sbuf_tensor("mn_f", [128, 8], F32))
        gate_sb = ctx.enter_context(nc.sbuf_tensor("gate_sb", [128, 1], F32))
        pm_sb = ctx.enter_context(nc.sbuf_tensor("pm_sb", [128, 4], F32))
        rb_sb = ctx.enter_context(nc.sbuf_tensor("rb_sb", [128, 8], F32))
        coef = ctx.enter_context(nc.sbuf_tensor("coef", [128, 8], F32))
        stat = ctx.enter_context(nc.sbuf_tensor("stat", [128, 32], F32))
        mnred_bf = ctx.enter_context(nc.sbuf_tensor("mnred_bf", [128, 1], BF16))
        act_bf = ctx.enter_context(nc.sbuf_tensor("act_bf", [1, 1], BF16))
        zbuf = ctx.enter_context(nc.sbuf_tensor("zbuf", [128, 64], F32))
        zbf = ctx.enter_context(nc.sbuf_tensor("zbf", [128, 32], BF16))
        zt_bf = ctx.enter_context(nc.sbuf_tensor("zt_bf", [128, 8], BF16))
        rrow = ctx.enter_context(nc.sbuf_tensor("rrow", [128, NSUB * 8], F32))
        rrow2 = ctx.enter_context(nc.sbuf_tensor("rrow2", [128, NSUB * 8], F32))
        rsml = ctx.enter_context(nc.sbuf_tensor("rsml", [128, NSUB * 3], F32))
        m8buf = ctx.enter_context(nc.sbuf_tensor("m8buf", [128, 8], F32))


        PA = ctx.enter_context(nc.psum_tensor("PA", [128, D], F32))
        PB = ctx.enter_context(nc.psum_tensor("PB", [128, D], F32))
        PC = ctx.enter_context(nc.psum_tensor("PC", [128, D], F32))
        ps_guard = nc.psum_tensor("PS", [128, D], F32)
        PS = ps_guard.__enter__()

        dg_v = u_big.ap().bitcast(BF16)

        def dG(c):
            return dg_v[:, 8192 * c: 8192 * (c + 1)]
        G_own = dg_v[:, 24576: 32768]
        ring_v = dg_v[:, 24576: 32768]

        def tmem_ring(i):
            return ring_v[:, 1024 * (i % 2): 1024 * (i % 2 + 1)]

        def oc_ring(i):
            return ring_v[:, 2048 + 1024 * (i % 4): 2048 + 1024 * (i % 4 + 1)]

        x_tm = u_big.ap()
        h1_tm = u_v.ap()
        v_tm = u_v.ap()
        h1_fm = u_a.ap()
        sq_fm = u_a.ap()
        oc_fm = u_a.ap()
        h2_fm = u_a.ap()
        wk_sb = u_b.ap()[:, 0:8192]
        wv_sb = u_b.ap()[:, 8192:16384]
        wq_sb = u_c.ap()[:, 0:8192]
        sk_fm = u_b.ap()
        mem_sb = u_b.ap()[:, 0:8192]
        wo_sb = u_b.ap()[:, 8192:16384]
        basew_sb = u_b.ap()[:, 0:8192]
        S_run = u_c.ap()[:, 0:8192]
        Ht_sb = u_c.ap()[:, 0:NSUB * 118]
        Hp_tm = u_c.ap()[:, 2048:2048 + NSUB * 128]
        Hp_fm = u_c.ap()[:, 4096:4096 + T]
        acat_sb = u_c.ap()[:, 6144:6144 + 8 * 118]
        bcat_sb = u_c.ap()[:, 7088:7088 + D]
        skst_r = u_c.ap()[:, 8192:12288]
        sqst_r = u_c.ap()[:, 12288:16384]
        AT_sb = u_c.ap()[:, 8192:16384]

        def hfm_ring(i):
            return u_c.ap()[:, 8192 + 1024 * (i % 2): 8192 + 1024 * (i % 2 + 1)]

        def h2st(i):
            return u_c.ap()[:, 12288 + 1024 * (i % 4): 12288 + 1024 * (i % 4 + 1)]

        def chunk_of(s_):
            return s_ // SPC

        P.op("gpsimd", lambda g: g.memset(masks[:], 1.0))
        P.op("gpsimd", lambda g: g.affine_select(
            masks[:], masks[:], pattern=[[1, 896]], compare_op=A.is_ge,
            fill=0.0, base=-384, channel_multiplier=-1), drain=True)
        P.op("gpsimd", lambda g: g.memset(ident[:], 1.0), drain=True)
        P.op("gpsimd", lambda g: g.affine_select(
            ident[:], ident[:], pattern=[[1, 128]], compare_op=A.is_equal,
            fill=0.0, base=0, channel_multiplier=-1), drain=True)
        P.op("gpsimd", lambda g: g.memset(ones_col[:], 1.0), drain=True)
        P.op("gpsimd", lambda g: g.memset(ones_row[:], 1.0), drain=True,
             inc=("GPC", 1))

        P.op("sync", lambda s: s.dma_start(out=mn_f[:], in_=mn_c), inc=("SYS", 16))
        P.op("sync", lambda s: s.dma_start(out=gate_sb[:], in_=gate_c),
             inc=("SYS", 16))
        P.op("sync", lambda s: s.dma_start(out=pm_sb[:], in_=pmask), inc=("SYS", 16))
        P.op("sync", lambda s: s.dma_start(out=rb_sb[:], in_=rb_c), inc=("SYS", 16))

        def wload(dst, src):
            return lambda g: g.dma_start(
                out=dst.rearrange("p (t e) -> p t e", t=DT),
                in_=src.rearrange("(t p) e -> p t e", p=128))
        P.op("gpsimd", wload(wk_sb, wk_t), inc=("GWKV", 16))
        P.op("gpsimd", wload(wv_sb, wv_t), inc=("GWKV", 16))
        P.op("gpsimd", lambda g: g.dma_start(out=mn_bf[:], in_=mn_c),
             inc=("GWKV", 16))
        P.wait("gpsimd", "SYH1a0", 64)
        P.wait("gpsimd", "SYH1b0", 64)
        P.op("gpsimd", wload(wq_sb, wq_t), inc=("GWQ", 16))

        for s_ in range(2 * SPC if NSUB > 2 * SPC else NSUB):
            qn = "sync" if chunk_of(s_) == 0 else "scalar"
            P.op(qn, lambda s, s_=s_: s.dma_start(
                out=x_tm[:, D * s_:D * (s_ + 1)],
                in_=xseg[128 * s_:128 * (s_ + 1), :]),
                inc=("SYX%d" % chunk_of(s_), 16))

        P.wait("vector", "SYS", 64)
        P.wait("scalar", "SYS", 64)
        P.op("scalar", lambda s: s.activation(out=coef[:, 0:1], in_=gate_sb[:],
                                              func=AF.Sigmoid), inc=("SA", 1))
        P.wait("vector", "SA", P.mark("SA"))
        P.op("vector", lambda v: v.tensor_scalar(coef[:, 1:2], coef[:, 0:1],
                                                 -1.0, 1.0, A.mult, A.add),
             drain=True)
        P.op("vector", lambda v: v.reduce_sum(stat[:, 23:24], mn_f[:], axis=AX),
             drain=True)
        P.op("vector", lambda v: v.tensor_copy(mnred_bf[:], stat[:, 23:24]),
             drain=True, inc=("VE", 1))
        P.wait("tensor", "VE", P.mark("VE"))
        P.wait("tensor", "GPC", 1)
        P.op("tensor", lambda t: t.matmul(PS[0:1, 16:17], mnred_bf[:], ones_col[:],
                                          start=True, stop=True), inc=("PE", 1))
        P.wait("vector", "PE", P.mark("PE"))
        P.op("vector", lambda v: v.tensor_scalar(act_bf[0:1, :], PS[0:1, 16:17],
                                                 float(EPS), None, A.is_ge),
             drain=True, inc=("VE", 1))
        P.wait("tensor", "VE", P.mark("VE"))
        P.op("tensor", lambda t: t.matmul(PS[:, 24:25], ones_row[:], act_bf[0:1, :],
                                          start=True, stop=True), inc=("PE", 1))
        P.wait("vector", "PE", P.mark("PE"))
        P.op("vector", lambda v: v.tensor_copy(coef[:, 2:3], PS[:, 24:25]),
             drain=True)
        P.op("vector", lambda v: v.tensor_tensor(coef[:, 3:4], coef[:, 0:1],
                                                 coef[:, 2:3], A.mult), drain=True,
             inc=("VE", 1))
        mk_coef = P.mark("VE")

        def ln_cols(par):
            b = 16 * par
            return (stat[:, b:b + 6], stat[:, b + 6:b + 12],
                    stat[:, b + 12:b + 14], stat[:, b + 14:b + 15],
                    stat[:, b + 15:b + 16])

        def emit_ln_a(xin, par, wait_ve=None):
            c0, c1, mv, t_, r_ = ln_cols(par)
            if wait_ve is not None:
                P.wait("vector", "VE", wait_ve)
            P.op("vector", lambda v: v.bn_stats(c0, xin[:, 0:512]), drain=True)
            P.op("vector", lambda v: v.bn_stats(c1, xin[:, 512:1024]), drain=True)
            P.op("vector", lambda v: v.bn_aggr(mv, stat[:, 16 * par:16 * par + 12]),
                 drain=True)
            P.op("vector", lambda v: v.tensor_scalar_add(t_, mv[:, 1:2], LN_EPS),
                 drain=True, inc=("VE", 1))
            return P.mark("VE")

        def emit_ln_b(par, mk_a):
            c0, c1, mv, t_, r_ = ln_cols(par)
            P.wait("scalar", "VE", mk_a)
            P.op("scalar", lambda s: s.activation(out=t_, in_=t_, func=AF.Ln),
                 drain=True)
            P.op("scalar", lambda s: s.activation(out=r_, in_=t_, func=AF.Exp,
                                                  scale=-0.5), drain=True,
                 inc=("SA", 1))
            return P.mark("SA")

        def emit_ln_c(xin, out_ap, par, mk_b, wait_ve=None):
            c0, c1, mv, t_, r_ = ln_cols(par)
            P.wait("vector", "SA", mk_b)
            if wait_ve is not None:
                P.wait("vector", "VE", wait_ve)
            P.op("vector", lambda v: v.tensor_scalar(
                out_ap, xin, mv[:, 0:1], r_, A.subtract, A.mult),
                drain=True, inc=("VE", 1))
            return P.mark("VE")

        pend = None
        for s_ in range(NSUB):
            c = chunk_of(s_)
            if s_ % SPC == 0:
                if c + 2 < NCH:
                    for si in range(SPC * (c + 2), SPC * (c + 3)):
                        P.op("scalar", lambda s, si=si: s.dma_start(
                            out=x_tm[:, D * si:D * (si + 1)],
                            in_=xseg[128 * si:128 * (si + 1), :]),
                            inc=("SYX%d" % (c + 2), 16))
                P.wait("vector", "SYX%d" % c, 64)
            mka = emit_ln_a(x_tm[:, D * s_:D * (s_ + 1)], s_ % 2)
            mkb = emit_ln_b(s_ % 2, mka)
            if pend is not None:
                ps_, pmk = pend
                mk = emit_ln_c(x_tm[:, D * ps_:D * (ps_ + 1)],
                               h1_tm[:, D * ps_:D * (ps_ + 1)], ps_ % 2, pmk)
                P.wait("scalar", "VE", mk)
                P.op("scalar", lambda s, ps_=ps_: s.dma_start(
                    out=scr_h1[128 * ps_:128 * (ps_ + 1), :],
                    in_=h1_tm[:, D * ps_:D * (ps_ + 1)]),
                    inc=("SH1%d" % chunk_of(ps_), 16))
            pend = (s_, mkb)
        ps_, pmk = pend
        mk = emit_ln_c(x_tm[:, D * ps_:D * (ps_ + 1)],
                       h1_tm[:, D * ps_:D * (ps_ + 1)], ps_ % 2, pmk)
        P.wait("scalar", "VE", mk)
        P.op("scalar", lambda s, ps_=ps_: s.dma_start(
            out=scr_h1[128 * ps_:128 * (ps_ + 1), :],
            in_=h1_tm[:, D * ps_:D * (ps_ + 1)]),
            inc=("SH1%d" % chunk_of(ps_), 16))

        for c in range(NCH):
            P.wait("sync", "SH1%d" % c, 64)
            for dt in range(4):
                P.op("sync", lambda s, c=c, dt=dt: s.dma_start(
                    out=h1_fm[:, 2048 * dt + CH * c: 2048 * dt + CH * (c + 1)],
                    in_=scr_h1[CH * c:CH * (c + 1), 128 * dt:128 * (dt + 1)],
                    transpose=True), inc=("SYH1a%d" % c, 16))
            P.wait("scalar", "SH1%d" % c, 64)
            for dt in range(4, DT):
                P.op("scalar", lambda s, c=c, dt=dt: s.dma_start(
                    out=h1_fm[:, 2048 * dt + CH * c: 2048 * dt + CH * (c + 1)],
                    in_=scr_h1[CH * c:CH * (c + 1), 128 * dt:128 * (dt + 1)],
                    transpose=True), inc=("SYH1b%d" % c, 16))

        P.wait("tensor", "GWKV", 48)
        P.wait("tensor", "GPC", 1)
        q_ev = {}
        k_ev = {}
        v_ev = {}
        prev_evac = {0: (0, 0), 1: (0, 0)}
        last_stt = 0
        kv_pe_c = {}
        for c in range(NCH):
            P.wait("tensor", "SYH1a%d" % c, 64)
            P.wait("tensor", "SYH1b%d" % c, 64)
            for k_ in range(SPC):
                s_ = SPC * c + k_
                kst = skst_r[:, 1024 * k_:1024 * (k_ + 1)]
                for h in range(2):
                    hs = slice(512 * h, 512 * (h + 1))
                    P.wait("tensor", "VE", prev_evac[h][0])
                    P.wait("tensor", "SA", prev_evac[h][1])
                    for dt in range(DT):
                        lo = 2048 * dt + 128 * s_
                        lhsT = h1_fm[:, lo:lo + 128]
                        for (ps, w_sb) in ((PB, wk_sb), (PC, wv_sb)):
                            P.op("tensor", lambda t, ps=ps, lhsT=lhsT, w_sb=w_sb,
                                 dt=dt, h=h, hs=hs: t.matmul(
                                     ps[:, hs], lhsT,
                                     w_sb[:, 1024 * dt + 512 * h:
                                          1024 * dt + 512 * (h + 1)],
                                     start=(dt == 0), stop=(dt == DT - 1)))
                    P.inc_last("tensor", "PE", 1)
                    mk_pe = P.mark("PE")
                    P.wait("scalar", "PE", mk_pe)
                    P.wait("scalar", "VE", last_stt)
                    P.op("scalar", lambda s, hs=hs: s.activation(
                        out=me_sb[:, 0:512], in_=PB[:, hs], func=AF.Relu,
                        scale=-1.0), drain=True)
                    P.op("scalar", lambda s: s.activation(
                        out=me_sb[:, 512:1024], in_=me_sb[:, 0:512], func=AF.Exp,
                        scale=-1.0), drain=True, inc=("SA", 1))
                    P.wait("vector", "SA", P.mark("SA"))
                    P.wait("vector", "PE", mk_pe)
                    if c > 0 and h == 0 and k_ == 0:
                        P.wait("vector", "SKW%d" % (c - 1), 64)
                    P.op("vector", lambda v, kst=kst, hs=hs: v.scalar_tensor_tensor(
                        kst[:, hs], PB[:, hs], 0.0, me_sb[:, 512:1024],
                        A.max, A.add), drain=True, inc=("VE", 1))
                    k_ev[(s_, h)] = P.mark("VE")
                    last_stt = k_ev[(s_, h)]
                    if h == 0:
                        P.wait("scalar", "SH1%d" % c, 64)
                    P.op("scalar", lambda s, s_=s_, hs=hs: s.activation(
                        out=v_tm[:, D * s_:D * (s_ + 1)][:, hs],
                        in_=PC[:, hs], func=AF.Copy), drain=True, inc=("SA", 1))
                    v_ev[(s_, h)] = P.mark("SA")
                    prev_evac[h] = (k_ev[(s_, h)], v_ev[(s_, h)])
                P.wait("scalar", "VE", k_ev[(s_, 1)])
                P.op("scalar", lambda s, s_=s_, kst=kst: s.dma_start(
                    out=scr_k[128 * s_:128 * (s_ + 1), :], in_=kst),
                    inc=("SKW%d" % c, 16))
            P.wait("tensor", "SA", v_ev[(SPC * c + SPC - 1, 1)])
            P.wait("tensor", "VE", k_ev[(SPC * c + SPC - 1, 1)])
            dg_marks = {}
            for dt in range(DT):
                ps = PC if dt % 2 == 0 else PA
                for j in range(SPC):
                    lhsT = skst_r[:, 1024 * j + 128 * dt:1024 * j + 128 * (dt + 1)]
                    sj = SPC * c + j
                    for h in range(2):
                        P.op("tensor", lambda t, ps=ps, lhsT=lhsT, sj=sj, h=h, j=j:
                             t.matmul(ps[:, 512 * h:512 * (h + 1)], lhsT,
                                      v_tm[:, D * sj + 512 * h:
                                           D * sj + 512 * (h + 1)],
                                      start=(j == 0), stop=(j == SPC - 1)))
                    P.op("tensor", lambda t, lhsT=lhsT, dt=dt, j=j: t.matmul(
                        PS[:, dt:dt + 1], lhsT, ones_col[:],
                        start=(j == 0), stop=(j == SPC - 1)))
                P.inc_last("tensor", "PE", 1)
                mk = P.mark("PE")
                if c < NCH - 1:
                    P.wait("scalar", "PE", mk)
                    P.op("scalar", lambda s, c=c, dt=dt, ps=ps: s.activation(
                        out=dG(c)[:, 1024 * dt:1024 * (dt + 1)], in_=ps[:],
                        func=AF.Copy), drain=True, inc=("SA", 1))
                    P.wait("vector", "SA", P.mark("SA"))
                    if c == 0:
                        P.op("vector", lambda v, c=c, dt=dt: v.tensor_copy(
                            G_own[:, 1024 * dt:1024 * (dt + 1)],
                            dG(c)[:, 1024 * dt:1024 * (dt + 1)]), drain=True,
                            inc=("VE", 1))
                    else:
                        P.op("vector", lambda v, c=c, dt=dt: v.tensor_tensor(
                            G_own[:, 1024 * dt:1024 * (dt + 1)],
                            G_own[:, 1024 * dt:1024 * (dt + 1)],
                            dG(c)[:, 1024 * dt:1024 * (dt + 1)], A.add),
                            drain=True, inc=("VE", 1))
                else:
                    P.wait("vector", "PE", mk)
                    if c == 0:
                        P.op("vector", lambda v, dt=dt, ps=ps: v.tensor_copy(
                            G_own[:, 1024 * dt:1024 * (dt + 1)], ps[:]),
                            drain=True, inc=("VE", 1))
                    else:
                        P.op("vector", lambda v, dt=dt, ps=ps: v.tensor_tensor(
                            G_own[:, 1024 * dt:1024 * (dt + 1)],
                            G_own[:, 1024 * dt:1024 * (dt + 1)], ps[:], A.add),
                            drain=True, inc=("VE", 1))
                dg_marks[dt % 2] = (P.mark("VE"), P.mark("SA"))
                if dt + 1 < DT:
                    nxt = dg_marks.get((dt + 1) % 2)
                    if nxt is not None:
                        P.wait("tensor", "SA", nxt[1])
                        P.wait("tensor", "VE", nxt[0])
            P.wait("vector", "PE", P.mark("PE"))
            P.op("vector", lambda v, c=c: v.tensor_copy(
                zbuf[:, 8 * (c + 1):8 * (c + 2)], PS[:, 0:8]), drain=True,
                inc=("VE", 1))
            prev_evac = {0: (P.mark("VE"), P.mark("SA")),
                         1: (P.mark("VE"), P.mark("SA"))}
            kv_pe_c[c] = P.mark("PE")
        mk_kv_pe = P.mark("PE")

        P.op("vector", lambda v: v.tensor_copy(zbuf[:, 40:48], zbuf[:, 8:16]),
             drain=True)
        for c in range(1, NCH):
            P.op("vector", lambda v, c=c: v.tensor_tensor(
                zbuf[:, 40:48], zbuf[:, 40:48],
                zbuf[:, 8 * (c + 1):8 * (c + 2)], A.add), drain=True)
        P.op("vector", lambda v: v.tensor_copy(zt_bf[:], zbuf[:, 40:48]),
             drain=True, inc=("VE", 1))
        P.wait("gpsimd", "VE", P.mark("VE"))
        P.op("gpsimd", lambda g: g.dma_start(
            out=gz[0:1024, :].rearrange("(t p) e -> p t e", p=128),
            in_=G_own.rearrange("p (t e) -> p t e", t=DT)), inc=("GGZ", 16))
        P.op("gpsimd", lambda g: g.dma_start(
            out=gz[1024:1025, :].rearrange("r (t p) -> r t p", t=8).transpose(
                [2, 0, 1]),
            in_=zt_bf[:].rearrange("p (r t) -> p r t", r=1),
            allow_slow_non_contiguous=True), inc=("GGZ", 16))
        P.wait("gpsimd", "GGZ", P.mark("GGZ"))
        P.op("gpsimd", lambda g: g.collective_compute(
            "AllGather", A.bypass, replica_groups=[[0, 1, 2, 3], [4, 5, 6, 7]],
            ins=[gz.opt()], outs=[gzall.opt()]), inc=("COL", 1))

        P.wait("sync", "PE", mk_kv_pe)
        for c in range(NCH):
            P.wait("sync", "SKW%d" % c, 64)
            for dt in range(DT):
                P.op("sync", lambda s, c=c, dt=dt: s.dma_start(
                    out=sk_fm[:, 2048 * dt + CH * c: 2048 * dt + CH * (c + 1)],
                    in_=scr_k[CH * c:CH * (c + 1), 128 * dt:128 * (dt + 1)],
                    transpose=True), inc=("SYKQ%d" % c, 16))
        for c in range(NCH):
            P.wait("sync", "SQW%d" % c, 64)
            for dt in range(DT):
                P.op("sync", lambda s, c=c, dt=dt: s.dma_start(
                    out=sq_fm[:, 2048 * dt + CH * c: 2048 * dt + CH * (c + 1)],
                    in_=scr_q[CH * c:CH * (c + 1), 128 * dt:128 * (dt + 1)],
                    transpose=True), inc=("SQRD%d" % c, 16))

        at_bank = {0: 0, 1: 0}
        at_done = {}
        at_guard = [True, True]

        def emit_at_chunk(c):
            if 2048 * c < 4096 and at_guard[0]:
                for cc in range(NCH):
                    P.wait("vector", "SKW%d" % cc, 64)
                at_guard[0] = False
            if 2048 * c >= 4096 and at_guard[1]:
                for cc in range(NCH):
                    P.wait("vector", "SQW%d" % cc, 64)
                at_guard[1] = False
            P.wait("tensor", "SYKQ%d" % c, 128)
            P.wait("tensor", "SQRD%d" % c, 128)
            for j in range(SPC):
                bank = j % 2
                ps = PB if bank == 0 else PC
                P.wait("tensor", "VE", at_bank[bank])
                for dt in range(DT):
                    P.op("tensor", lambda t, ps=ps, c=c, j=j, dt=dt: t.matmul(
                        ps[:, 0:512],
                        sk_fm[:, 2048 * dt + CH * c + 128 * j:
                              2048 * dt + CH * c + 128 * (j + 1)],
                        sq_fm[:, 2048 * dt + CH * c: 2048 * dt + CH * (c + 1)],
                        start=(dt == 0), stop=(dt == DT - 1)))
                P.inc_last("tensor", "PE", 1)
                P.wait("vector", "PE", P.mark("PE"))
                P.op("vector", lambda v, ps=ps, c=c, j=j: v.tensor_tensor(
                    AT_sb[:, 2048 * c + 512 * j: 2048 * c + 512 * (j + 1)],
                    ps[:, 0:512], masks[:, 384 - 128 * j:896 - 128 * j], A.mult),
                    drain=True, inc=("VE", 1))
                at_bank[bank] = P.mark("VE")
            at_done[c] = P.mark("VE")

        P.wait("tensor", "GWQ", 16)
        mk_kv_ve = P.mark("VE")
        prev_q = {0: (mk_kv_ve, 0), 1: (mk_kv_ve, 0)}
        at_bank[0] = mk_kv_ve
        at_bank[1] = mk_kv_ve
        q_pe_c = {}
        for c in range(NCH):
            for k_ in range(SPC):
                s_ = SPC * c + k_
                qst = sqst_r[:, 1024 * k_:1024 * (k_ + 1)]
                for h in range(2):
                    hs = slice(512 * h, 512 * (h + 1))
                    P.wait("tensor", "VE", prev_q[h][0])
                    for dt in range(DT):
                        lo = 2048 * dt + 128 * s_
                        lhsT = h1_fm[:, lo:lo + 128]
                        P.op("tensor", lambda t, lhsT=lhsT, dt=dt, h=h, hs=hs:
                             t.matmul(PA[:, hs], lhsT,
                                      wq_sb[:, 1024 * dt + 512 * h:
                                            1024 * dt + 512 * (h + 1)],
                                      start=(dt == 0), stop=(dt == DT - 1)))
                    P.inc_last("tensor", "PE", 1)
                    mk_pe = P.mark("PE")
                    P.wait("scalar", "PE", mk_pe)
                    P.wait("scalar", "VE", last_stt)
                    P.op("scalar", lambda s, hs=hs: s.activation(
                        out=me_sb[:, 0:512], in_=PA[:, hs], func=AF.Relu,
                        scale=-1.0), drain=True)
                    P.op("scalar", lambda s: s.activation(
                        out=me_sb[:, 512:1024], in_=me_sb[:, 0:512], func=AF.Exp,
                        scale=-1.0), drain=True, inc=("SA", 1))
                    P.wait("vector", "SA", P.mark("SA"))
                    P.wait("vector", "PE", mk_pe)
                    if c > 0 and h == 0 and k_ == 0:
                        P.wait("vector", "SQW%d" % (c - 1), 64)
                    P.op("vector", lambda v, qst=qst, hs=hs: v.scalar_tensor_tensor(
                        qst[:, hs], PA[:, hs], 0.0, me_sb[:, 512:1024],
                        A.max, A.add), drain=True, inc=("VE", 1))
                    q_ev[(s_, h)] = P.mark("VE")
                    last_stt = q_ev[(s_, h)]
                    prev_q[h] = (q_ev[(s_, h)], 0)
                P.wait("scalar", "VE", q_ev[(s_, 1)])
                P.op("scalar", lambda s, s_=s_, qst=qst: s.dma_start(
                    out=scr_q[128 * s_:128 * (s_ + 1), :], in_=qst),
                    inc=("SQW%d" % c, 16))
            q_pe_c[c] = P.mark("PE")
            if c >= 1:
                emit_at_chunk(c - 1)
        emit_at_chunk(NCH - 1)
        mk_p2_ve = P.mark("VE")
        mk_p2_pe = P.mark("PE")
        mk_at_pe = P.mark("PE")

        def gs_half(hf):
            return ring_v[:, 4096 * hf:4096 * (hf + 1)]

        P.wait("gpsimd", "COL", 1)
        P.wait("gpsimd", "GGZ", P.mark("GGZ"))
        P.wait("vector", "PE", mk_p2_pe)
        P.op("vector", lambda v: v.memset(S_run[:], 0.0), drain=True)
        P.op("vector", lambda v: v.memset(zbuf[:, 0:8], 0.0), drain=True,
             inc=("VE", 1))
        mk_pzero = P.mark("VE")
        half_ve = {0: mk_pzero, 1: mk_pzero}
        for r in range(3):
            P.op("gpsimd", lambda g, r=r: g.dma_start(
                out=zbf[:, 8 * r:8 * (r + 1)],
                in_=gzall[1025 * r + 1024:1025 * r + 1025, :].rearrange(
                    "rr (t p) -> rr t p", t=8).transpose([2, 0, 1]),
                allow_slow_non_contiguous=True), inc=("GPZ%d" % r, 16))
            for hf in range(2):
                P.wait("gpsimd", "VE", half_ve[hf])
                P.op("gpsimd", lambda g, r=r, hf=hf: g.dma_start(
                    out=gs_half(hf).rearrange("p (t e) -> p t e", t=4),
                    in_=gzall[1025 * r + 512 * hf:1025 * r + 512 * (hf + 1),
                              :].rearrange("(t p) e -> p t e", p=128)),
                    inc=("GPR%d_%d" % (r, hf), 16))
                P.wait("vector", "GPR%d_%d" % (r, hf), 16)
                P.op("vector", lambda v, r=r, hf=hf: v.scalar_tensor_tensor(
                    S_run[:, 4096 * hf:4096 * (hf + 1)], gs_half(hf)[:],
                    pm_sb[:, r:r + 1], S_run[:, 4096 * hf:4096 * (hf + 1)],
                    A.mult, A.add), drain=True, inc=("VE", 1))
                half_ve[hf] = P.mark("VE")
            P.wait("vector", "GPZ%d" % r, 16)
            P.op("vector", lambda v, r=r: v.scalar_tensor_tensor(
                zbuf[:, 0:8], zbf[:, 8 * r:8 * (r + 1)], pm_sb[:, r:r + 1],
                zbuf[:, 0:8], A.mult, A.add), drain=True, inc=("VE", 1))
        for c in range(NCH):
            if c > 0:
                P.op("vector", lambda v, c=c: v.tensor_tensor(
                    zbuf[:, 0:8], zbuf[:, 0:8], zbuf[:, 8 * c:8 * (c + 1)],
                    A.add), drain=True)
            P.op("vector", lambda v, c=c: v.tensor_copy(
                zbf[:, 8 * c:8 * (c + 1)], zbuf[:, 0:8]), drain=True)
        P.op("vector", lambda v: v.engine_nop(), inc=("VE", 1))
        mk_prefix = P.mark("VE")
        P.wait("gpsimd", "PE", mk_at_pe)
        P.op("gpsimd", wload(mem_sb, memw), inc=("GME", 16))
        P.op("gpsimd", wload(wo_sb, wo_t), inc=("GWO", 16))

        def oc_slot(i):
            sl = i % 4
            if sl == 0:
                return me_sb[:]
            return dg_v[:, 24576 + 1024 * (sl - 1):24576 + 1024 * sl]

        P.wait("tensor", "VE", mk_prefix)
        P.wait("tensor", "GME", 16)
        prev_num = {0: mk_prefix, 1: mk_prefix}
        mk_sadd = mk_prefix
        prev_t_sa = P.mark("SA")
        prev_stat_ve = mk_prefix
        P.wait("vector", "VE", mk_coef)
        scan_pe_c = {}
        for c in range(NCH):
            for k_ in range(SPC):
                s_ = SPC * c + k_
                nb = s_ % 2
                nps = PA if nb == 0 else PC
                P.wait("tensor", "VE", prev_num[nb])
                for j in range(k_ + 1):
                    lhsT = AT_sb[:, 2048 * c + 512 * j + 128 * k_:
                                 2048 * c + 512 * j + 128 * (k_ + 1)]
                    for h in range(2):
                        P.op("tensor", lambda t, nps=nps, lhsT=lhsT, c=c, j=j,
                             h=h: t.matmul(
                                 nps[:, 512 * h:512 * (h + 1)], lhsT,
                                 v_tm[:, D * (SPC * c + j) + 512 * h:
                                      D * (SPC * c + j) + 512 * (h + 1)],
                                 start=(j == 0), stop=False))
                P.wait("tensor", "SA", prev_t_sa)
                if k_ == 0 and c > 0:
                    for dt in range(DT):
                        lhsT = sq_fm[:, 2048 * dt + 128 * s_:
                                     2048 * dt + 128 * (s_ + 1)]
                        for h in range(2):
                            P.op("tensor", lambda t, lhsT=lhsT, dt=dt, h=h:
                                 t.matmul(PB[:, 512 * h:512 * (h + 1)], lhsT,
                                          mem_sb[:, 1024 * dt + 512 * h:
                                                 1024 * dt + 512 * (h + 1)],
                                          start=(dt == 0), stop=(dt == DT - 1)))
                    P.wait("tensor", "VE", mk_sadd)
                    for dt in range(DT):
                        lhsT = sq_fm[:, 2048 * dt + 128 * s_:
                                     2048 * dt + 128 * (s_ + 1)]
                        for h in range(2):
                            P.op("tensor", lambda t, nps=nps, lhsT=lhsT, dt=dt,
                                 h=h: t.matmul(
                                     nps[:, 512 * h:512 * (h + 1)], lhsT,
                                     S_run[:, 1024 * dt + 512 * h:
                                           1024 * dt + 512 * (h + 1)],
                                     start=False, stop=(dt == DT - 1)))
                else:
                    for dt in range(DT):
                        lhsT = sq_fm[:, 2048 * dt + 128 * s_:
                                     2048 * dt + 128 * (s_ + 1)]
                        for h in range(2):
                            P.op("tensor", lambda t, nps=nps, lhsT=lhsT, dt=dt,
                                 h=h: t.matmul(
                                     nps[:, 512 * h:512 * (h + 1)], lhsT,
                                     S_run[:, 1024 * dt + 512 * h:
                                           1024 * dt + 512 * (h + 1)],
                                     start=False, stop=(dt == DT - 1)))
                            P.op("tensor", lambda t, lhsT=lhsT, dt=dt, h=h:
                                 t.matmul(PB[:, 512 * h:512 * (h + 1)], lhsT,
                                          mem_sb[:, 1024 * dt + 512 * h:
                                                 1024 * dt + 512 * (h + 1)],
                                          start=(dt == 0), stop=(dt == DT - 1)))
                P.wait("tensor", "VE", prev_stat_ve)
                for j in range(k_ + 1):
                    lhsT = AT_sb[:, 2048 * c + 512 * j + 128 * k_:
                                 2048 * c + 512 * j + 128 * (k_ + 1)]
                    P.op("tensor", lambda t, lhsT=lhsT, j=j: t.matmul(
                        PS[:, 0:1], lhsT, ones_col[:], start=(j == 0), stop=False))
                for dt in range(DT):
                    lhsT = sq_fm[:, 2048 * dt + 128 * s_:2048 * dt + 128 * (s_ + 1)]
                    P.op("tensor", lambda t, lhsT=lhsT, c=c, dt=dt: t.matmul(
                        PS[:, 0:1], lhsT, zbf[:, 8 * c + dt:8 * c + dt + 1],
                        start=False, stop=(dt == DT - 1)))
                    P.op("tensor", lambda t, lhsT=lhsT, dt=dt: t.matmul(
                        PS[:, 512:513], lhsT, mn_bf[:, dt:dt + 1],
                        start=(dt == 0), stop=(dt == DT - 1)))
                P.inc_last("tensor", "PE", 1)
                mk = P.mark("PE")
                P.wait("vector", "PE", mk)
                P.op("vector", lambda v: v.tensor_scalar_max(
                    stat[:, 17:18], PS[:, 0:1], EPS), drain=True)
                P.op("vector", lambda v: v.reciprocal(stat[:, 18:19],
                                                      stat[:, 17:18]), drain=True)
                P.op("vector", lambda v: v.tensor_tensor(
                    stat[:, 19:20], stat[:, 18:19], coef[:, 1:2], A.mult),
                    drain=True)
                P.op("vector", lambda v: v.tensor_scalar_max(
                    stat[:, 20:21], PS[:, 512:513], EPS), drain=True)
                P.op("vector", lambda v: v.reciprocal(stat[:, 21:22],
                                                      stat[:, 20:21]), drain=True)
                P.op("vector", lambda v: v.tensor_tensor(
                    stat[:, 22:23], stat[:, 21:22], coef[:, 3:4], A.mult),
                    drain=True, inc=("VE", 1))
                prev_stat_ve = P.mark("VE")
                P.wait("scalar", "VE", prev_stat_ve)
                if c > 0:
                    P.wait("scalar", "SOC%d" % (c - 1), 64)
                osl = oc_slot(s_)
                P.op("scalar", lambda s, osl=osl: s.activation(
                    out=osl, in_=PB[:], func=AF.Copy, scale=stat[:, 22:23]),
                    drain=True, inc=("SA", 1))
                prev_t_sa = P.mark("SA")
                P.wait("vector", "SA", prev_t_sa)
                P.op("vector", lambda v, nps=nps, osl=osl: v.scalar_tensor_tensor(
                    osl, nps[:], stat[:, 19:20], osl, A.mult, A.add),
                    drain=True, inc=("VE", 1))
                prev_num[nb] = P.mark("VE")
                P.wait("scalar", "VE", prev_num[nb])
                P.op("scalar", lambda s, s_=s_, osl=osl: s.dma_start(
                    out=scr_oc[128 * s_:128 * (s_ + 1), :], in_=osl),
                    inc=("SOC%d" % c, 16))
            if c < NCH - 1:
                P.op("vector", lambda v, c=c: v.tensor_tensor(
                    S_run[:], S_run[:], dG(c), A.add), drain=True, inc=("VE", 1))
                mk_sadd = P.mark("VE")
                P.wait("sync", "VE", mk_sadd)
                for si in range(SPC * c, SPC * (c + 1)):
                    P.op("sync", lambda s, si=si: s.dma_start(
                        out=x_tm[:, D * si:D * (si + 1)],
                        in_=xseg[128 * si:128 * (si + 1), :]),
                        inc=("SYX2R%d" % c, 16))
            scan_pe_c[c] = P.mark("PE")
        mk_scan_ve = P.mark("VE")
        mk_scan_pe = P.mark("PE")
        mk_scan_sa = P.mark("SA")

        oc_rd = {}
        for c in range(NCH):
            P.wait("sync", "PE", scan_pe_c[c])
            P.wait("sync", "SOC%d" % c, 64)
            for dt in range(DT):
                P.op("sync", lambda s, c=c, dt=dt: s.dma_start(
                    out=oc_fm[:, 2048 * dt + CH * c: 2048 * dt + CH * (c + 1)],
                    in_=scr_oc[CH * c:CH * (c + 1), 128 * dt:128 * (dt + 1)],
                    transpose=True), inc=("SYOC%d" % c, 16))
        P.wait("sync", "GGZ", 32)
        P.wait("sync", "VE", mk_scan_ve)
        P.wait("sync", "SA", mk_scan_sa)
        P.wait("sync", "SOC%d" % (NCH - 1), 64)
        for si in range(SPC * (NCH - 1), NSUB):
            P.op("sync", lambda s, si=si: s.dma_start(
                out=x_tm[:, D * si:D * (si + 1)],
                in_=xseg[128 * si:128 * (si + 1), :]), inc=("SYX2L", 16))
        P.wait("gpsimd", "PE", mk_scan_pe)
        P.op("gpsimd", wload(basew_sb, basew_t), inc=("GFW", 16))
        P.op("gpsimd", lambda g: g.dma_start(
            out=acat_sb.rearrange("p (t r) -> p t r", t=DT),
            in_=acat_t.rearrange("(t p) r -> p t r", p=128)), inc=("GAB", 16))
        P.op("gpsimd", lambda g: g.dma_start(
            out=bcat_sb[0:113, :], in_=bcat), inc=("GAB", 16))

        def x2_wait_for(s_):
            cc = chunk_of(s_)
            if cc < NCH - 1:
                return ("SYX2R%d" % cc, 64)
            return ("SYX2L", 16 * SPC)

        P.wait("tensor", "GWO", 16)
        x2_done = {}
        oproj_pe_c = {}
        bank_mark = {0: mk_scan_ve, 1: mk_scan_ve}
        pend = None

        def _ln2_fin(ps_, pmk):
            c2 = chunk_of(ps_)
            if c2 > 0:
                P.wait("vector", "SH2%d" % (c2 - 1), 64)
            mk = emit_ln_c(x_tm[:, D * ps_:D * (ps_ + 1)], h2st(ps_), ps_ % 2,
                           pmk)
            P.wait("scalar", "VE", mk)
            P.op("scalar", lambda s, ps_=ps_: s.dma_start(
                out=scr_h2[128 * ps_:128 * (ps_ + 1), :], in_=h2st(ps_)),
                inc=("SH2%d" % c2, 16))

        for c in range(NCH):
            P.wait("tensor", "SYOC%d" % c, 128)
            for k_ in range(SPC):
                s_ = SPC * c + k_
                bank = s_ % 2
                ps = PA if bank == 0 else PB
                P.wait("tensor", "VE", bank_mark[bank])
                for et in range(DT):
                    lhsT = oc_fm[:, 2048 * et + 128 * s_:2048 * et + 128 * (s_ + 1)]
                    for h in range(2):
                        P.op("tensor", lambda t, ps=ps, lhsT=lhsT, et=et, h=h:
                             t.matmul(ps[:, 512 * h:512 * (h + 1)], lhsT,
                                      wo_sb[:, 1024 * et + 512 * h:
                                            1024 * et + 512 * (h + 1)],
                                      start=(et == 0), stop=(et == DT - 1)))
                P.inc_last("tensor", "PE", 1)
                P.wait("vector", "PE", P.mark("PE"))
                wsem, wval = x2_wait_for(s_)
                P.wait("vector", wsem, wval)
                P.op("vector", lambda v, s_=s_, ps=ps: v.tensor_tensor(
                    x_tm[:, D * s_:D * (s_ + 1)], x_tm[:, D * s_:D * (s_ + 1)],
                    ps[:], A.add), drain=True, inc=("VE", 1))
                x2_done[s_] = P.mark("VE")
                bank_mark[bank] = x2_done[s_]
                mka = emit_ln_a(x_tm[:, D * s_:D * (s_ + 1)], s_ % 2)
                mkb = emit_ln_b(s_ % 2, mka)
                if pend is not None:
                    _ln2_fin(*pend)
                pend = (s_, mkb)
            oproj_pe_c[c] = P.mark("PE")
            if c > 0:
                cc = c - 1
                P.wait("sync", "PE", oproj_pe_c[cc])
                P.wait("sync", "SH2%d" % cc, 64)
                for dt in range(DT):
                    P.op("sync", lambda s, cc=cc, dt=dt: s.dma_start(
                        out=h2_fm[:, 2048 * dt + CH * cc:
                                  2048 * dt + CH * (cc + 1)],
                        in_=scr_h2[CH * cc:CH * (cc + 1),
                                   128 * dt:128 * (dt + 1)],
                        transpose=True), inc=("SYH2%d" % cc, 16))
        _ln2_fin(*pend)
        cc = NCH - 1
        P.wait("sync", "PE", oproj_pe_c[cc])
        P.wait("sync", "SH2%d" % cc, 64)
        for dt in range(DT):
            P.op("sync", lambda s, cc=cc, dt=dt: s.dma_start(
                out=h2_fm[:, 2048 * dt + CH * cc: 2048 * dt + CH * (cc + 1)],
                in_=scr_h2[CH * cc:CH * (cc + 1), 128 * dt:128 * (dt + 1)],
                transpose=True), inc=("SYH2%d" % cc, 16))

        p10_bank = {0: None, 1: None}

        def emit_p10_chunk(c10):
            if p10_bank[0] is None:
                p10_bank[0] = P.mark("VE")
                p10_bank[1] = P.mark("VE")
                P.wait("tensor", "GFW", 16)
            P.wait("tensor", "SYHP%d" % c10, 16)
            for s_ in range(SPC * c10, SPC * (c10 + 1)):
                bank = s_ % 2
                ps = PA if bank == 0 else PB
                P.wait("tensor", "VE", p10_bank[bank])
                for dt in range(DT):
                    lhsT = h2_fm[:, 2048 * dt + 128 * s_:
                                 2048 * dt + 128 * (s_ + 1)]
                    for h in range(2):
                        P.op("tensor", lambda t, ps=ps, lhsT=lhsT, dt=dt, h=h:
                             t.matmul(ps[:, 512 * h:512 * (h + 1)], lhsT,
                                      basew_sb[:, 1024 * dt + 512 * h:
                                               1024 * dt + 512 * (h + 1)],
                                      start=(dt == 0), stop=False))
                lhsT2 = Hp_fm[0:113, 128 * s_:128 * (s_ + 1)]
                for h in range(2):
                    P.op("tensor", lambda t, ps=ps, lhsT2=lhsT2, h=h: t.matmul(
                        ps[:, 512 * h:512 * (h + 1)], lhsT2,
                        bcat_sb[0:113, 512 * h:512 * (h + 1)],
                        start=False, stop=True))
                P.inc_last("tensor", "PE", 1)
                P.wait("vector", "PE", P.mark("PE"))
                P.op("vector", lambda v, s_=s_, ps=ps: v.tensor_tensor(
                    x_tm[:, D * s_:D * (s_ + 1)], x_tm[:, D * s_:D * (s_ + 1)],
                    ps[:], A.add), drain=True, inc=("VE", 1))
                p10_bank[bank] = P.mark("VE")
                P.wait("sync", "VE", p10_bank[bank])
                P.op("sync", lambda s, s_=s_: s.dma_start(
                    out=y[128 * s_:128 * (s_ + 1), :],
                    in_=x_tm[:, D * s_:D * (s_ + 1)]), inc=("SYO", 16))


        ps_guard.__exit__(None, None, None)
        pf_guard = nc.psum_tensor("PF", [128, 2048], BF16)
        PF = pf_guard.__enter__()

        P.wait("tensor", "GFW", 16)
        P.wait("tensor", "GAB", 32)
        HtV = Ht_sb.rearrange("p (s r) -> p s r", s=NSUB)
        rrV = rrow.rearrange("p (s r) -> p s r", s=NSUB)
        rr2V = rrow2.rearrange("p (s r) -> p s r", s=NSUB)
        HpV = Hp_tm.rearrange("p (s r) -> p s r", s=NSUB)
        ht_bank = {0: mk_scan_ve, 1: mk_scan_ve}
        h_sa_marks = {0: P.mark("SA"), 1: P.mark("SA")}
        for c in range(NCH):
            sl = slice(SPC * c, SPC * (c + 1))
            P.wait("tensor", "SYH2%d" % c, 128)
            hb = 512 * (c % 2)
            P.wait("tensor", "SA", h_sa_marks[c % 2])
            for dt in range(DT):
                P.op("tensor", lambda t, c=c, dt=dt, hb=hb: t.matmul(
                    PC[0:118, hb:hb + 512], acat_sb[:, 118 * dt:118 * (dt + 1)],
                    h2_fm[:, 2048 * dt + CH * c:2048 * dt + CH * (c + 1)],
                    start=(dt == 0), stop=(dt == DT - 1)))
            P.inc_last("tensor", "PE", 1)
            P.wait("scalar", "PE", P.mark("PE"))
            P.op("scalar", lambda s, c=c, hb=hb: s.activation(
                out=hfm_ring(c)[0:118, 0:512], in_=PC[0:118, hb:hb + 512],
                func=AF.Copy), drain=True, inc=("SA", 1))
            h_sa_marks[c % 2] = P.mark("SA")
            P.wait("tensor", "SA", h_sa_marks[c % 2])
            for k_ in range(SPC):
                s_ = SPC * c + k_
                bank = s_ % 2
                pf = PF[:, 1024 * bank:1024 * bank + 118]
                P.wait("tensor", "VE", ht_bank[bank])
                P.op("tensor", lambda t, c=c, k_=k_, pf=pf: t.transpose(
                    pf, hfm_ring(c)[0:118, 128 * k_:128 * (k_ + 1)],
                    ident[0:118, 0:118]), inc=("PE", 1))
                P.wait("vector", "PE", P.mark("PE"))
                P.op("vector", lambda v, s_=s_, pf=pf: v.tensor_copy(
                    Ht_sb[:, 118 * s_:118 * (s_ + 1)], pf), drain=True,
                    inc=("VE", 1))
                ht_bank[bank] = P.mark("VE")
            P.op("vector", lambda v, sl=sl: v.memset(rrV[:, sl, :], -1e9),
                 drain=True)
            P.op("vector", lambda v, sl=sl: v.tensor_tensor(
                rrV[:, sl, 0:6], HtV[:, sl, 112:118],
                rb_sb[:, 0:6].unsqueeze(1).to_broadcast([128, SPC, 6]), A.add),
                drain=True, inc=("VE", 1))
            P.wait("scalar", "VE", P.mark("VE"))
            P.op("scalar", lambda s, c=c: s.activation(
                out=rrow[:, 8 * SPC * c:8 * SPC * (c + 1)],
                in_=rrow[:, 8 * SPC * c:8 * SPC * (c + 1)], func=AF.Exp),
                drain=True, inc=("SA", 1))
            P.wait("vector", "SA", P.mark("SA"))
            css = slice(SPC * c, SPC * (c + 1))
            P.op("vector", lambda v, css=css: v.tensor_reduce(
                rsml[:, css].unsqueeze(2), rrV[:, css, :], axis=AX, op=A.add),
                drain=True)
            P.op("vector", lambda v, css=css: v.reciprocal(
                rsml[:, css], rsml[:, css]), drain=True)
            P.op("vector", lambda v, css=css: v.tensor_tensor(
                rrV[:, css, :], rrV[:, css, :],
                rsml[:, css].unsqueeze(2).to_broadcast([128, SPC, 8]), A.mult),
                drain=True)
            for s_ in range(SPC * c, SPC * (c + 1)):
                P.op("vector", lambda v, s_=s_: v.max(
                    out=m8buf[:], in_=rrow[:, 8 * s_:8 * (s_ + 1)]), drain=True)
                P.op("vector", lambda v, s_=s_: v.tensor_copy(
                    rsml[:, NSUB + s_:NSUB + s_ + 1], m8buf[:, 1:2]), drain=True)
            cms = slice(NSUB + SPC * c, NSUB + SPC * (c + 1))
            P.op("vector", lambda v, css=css, cms=cms: v.tensor_tensor(
                rr2V[:, css, :], rrV[:, css, :],
                rsml[:, cms].unsqueeze(2).to_broadcast([128, SPC, 8]), A.is_ge),
                drain=True)
            P.op("vector", lambda v, css=css: v.tensor_tensor(
                rr2V[:, css, :], rr2V[:, css, :], rrV[:, css, :], A.mult),
                drain=True)
            c2s = slice(2 * NSUB + SPC * c, 2 * NSUB + SPC * (c + 1))
            P.op("vector", lambda v, css=css, c2s=c2s: v.tensor_reduce(
                rsml[:, c2s].unsqueeze(2), rr2V[:, css, :], axis=AX, op=A.add),
                drain=True)
            P.op("vector", lambda v, c2s=c2s: v.tensor_scalar_add(
                rsml[:, c2s], rsml[:, c2s], EPS), drain=True)
            P.op("vector", lambda v, c2s=c2s: v.reciprocal(
                rsml[:, c2s], rsml[:, c2s]), drain=True)
            P.op("vector", lambda v, css=css, c2s=c2s: v.tensor_tensor(
                rr2V[:, css, :], rr2V[:, css, :],
                rsml[:, c2s].unsqueeze(2).to_broadcast([128, SPC, 8]), A.mult),
                drain=True)
            P.op("vector", lambda v, css=css: v.tensor_copy(
                HpV[:, css, 0:16], HtV[:, css, 0:16]), drain=True)
            for e_ in range(6):
                P.op("vector", lambda v, e_=e_, css=css: v.tensor_tensor(
                    HpV[:, css, 16 + 16 * e_:32 + 16 * e_],
                    HtV[:, css, 16 + 16 * e_:32 + 16 * e_],
                    rr2V[:, css, e_:e_ + 1].to_broadcast([128, SPC, 16]),
                    A.mult), drain=True)
            P.op("vector", lambda v, css=css: v.memset(HpV[:, css, 112:113], 1.0),
                 drain=True)
            P.op("vector", lambda v, css=css: v.memset(HpV[:, css, 113:128], 0.0),
                 drain=True, inc=("VE", 1))
            mk_hp = P.mark("VE")
            P.wait("scalar", "VE", mk_hp)
            P.op("scalar", lambda s, c=c, css=css: s.dma_start(
                out=scr_hp[CH * c:CH * (c + 1), :].rearrange(
                    "(s p) r -> p s r", p=128), in_=HpV[:, css, :]),
                inc=("SHP%d" % c, 16))
            P.wait("sync", "SHP%d" % c, 16)
            P.op("sync", lambda s, c=c: s.dma_start(
                out=Hp_fm[:, CH * c:CH * (c + 1)],
                in_=scr_hp[CH * c:CH * (c + 1), :], transpose=True),
                inc=("SYHP%d" % c, 16))
            if c >= 1:
                emit_p10_chunk(c - 1)

        emit_p10_chunk(NCH - 1)
        P.wait("sync", "SYO", P.mark("SYO"))

        pf_guard.__exit__(None, None, None)

        sems = {}
        for name in P.sem_names:
            sems[name] = ctx.enter_context(nc.semaphore(name))

        block = ctx.enter_context(nc.Block())

        @block.sync
        def _(eng):
            P.replay("sync", eng, sems)

        @block.scalar
        def _(eng):
            P.replay("scalar", eng, sems)

        @block.vector
        def _(eng):
            P.replay("vector", eng, sems)

        @block.tensor
        def _(eng):
            P.replay("tensor", eng, sems)

        @block.gpsimd
        def _(eng):
            P.replay("gpsimd", eng, sems)

    return nc


def _stage_inputs(inputs, T=2048):
    x = np.asarray(inputs["L0_attn_hidden_states"], np.float32)
    tp = lambda w: np.ascontiguousarray(np.asarray(w, np.float32).T)
    mn = np.asarray(inputs["L0_attn_memory_norm"], np.float32)
    sa = np.asarray(inputs["L0_ffn_shared_A"], np.float32)
    ra = np.asarray(inputs["L0_ffn_routed_A"], np.float32)
    rw = np.asarray(inputs["L0_ffn_router_W"], np.float32)
    acat_t = np.ascontiguousarray(
        np.concatenate([sa, ra.reshape(96, 1024), rw], axis=0).T)
    sb = np.asarray(inputs["L0_ffn_shared_B"], np.float32)
    rbm = np.asarray(inputs["L0_ffn_routed_B"], np.float32)
    bb = np.asarray(inputs["L0_ffn_base_b"], np.float32)
    bcat = np.ascontiguousarray(np.concatenate(
        [np.ascontiguousarray(sb.T), rbm.transpose(0, 2, 1).reshape(96, 1024),
         bb[None, :]], axis=0))
    rbias = np.asarray(inputs["L0_ffn_router_b"], np.float32)
    rb_c = np.zeros((128, 8), np.float32)
    rb_c[:, 0:6] = rbias[None, :]
    shared = dict(
        wq_t=tp(inputs["L0_attn_w_q"]), wk_t=tp(inputs["L0_attn_w_k"]),
        wv_t=tp(inputs["L0_attn_w_v"]), wo_t=tp(inputs["L0_attn_w_o"]),
        basew_t=tp(inputs["L0_ffn_base_W"]),
        memw=np.ascontiguousarray(np.asarray(inputs["L0_attn_memory"],
                                             np.float32)),
        mn_c=np.ascontiguousarray(mn.reshape(8, 128).T),
        gate_c=np.ascontiguousarray(np.broadcast_to(
            np.asarray(inputs["L0_attn_gate"], np.float32).reshape(1, 1),
            (128, 1))),
        acat_t=acat_t, bcat=bcat, rb_c=rb_c)
    per = []
    for c in range(8):
        b, q = c // 4, c % 4
        pm = np.zeros((128, 4), np.float32)
        pm[:, 0:q] = 1.0
        m = dict(shared)
        m["xseg"] = np.ascontiguousarray(x[b, T * q:T * (q + 1), :])
        m["pmask"] = pm
        per.append(m)
    return per


def kernel(**inputs):
    T = 2048
    per = _stage_inputs(inputs, T)
    nc = build_nc(T)
    res = run_bass_kernel_spmd(nc, per, core_ids=list(range(8)))
    outs = [res.results[c]["y"] for c in range(8)]
    b0 = np.concatenate(outs[0:4], axis=0)
    b1 = np.concatenate(outs[4:8], axis=0)
    return np.stack([b0, b1], axis=0).astype(np.float32)
